# version 04
# baseline (speedup 1.0000x reference)
import numpy as np
import ml_dtypes
from contextlib import ExitStack

import concourse.bass as bass
import concourse.mybir as mybir
from concourse.bass_utils import run_bass_kernel_spmd

F32 = mybir.dt.float32
BF16 = mybir.dt.bfloat16
AF = mybir.ActivationFunctionType
ALU = mybir.AluOpType

NCORE = 8
USE_SHARED = False
B = 8
D = 1024
HF = 512
NE = 8
BF = ml_dtypes.bfloat16
TILE_ORDER = [(0, 3), (0, 2), (1, 3), (1, 2)]


def _strips_desc(c):
    return [31 - 2 * c, 30 - 2 * c, 2 * c + 1, 2 * c]


def _strip_owner(s):
    if s < 16:
        return s // 2, s % 2
    return (31 - s) // 2, 2 + (s % 2)


def _blk(s):
    c, l = _strip_owner(s)
    return c * 4 + l


class SemT:
    def __init__(self, nc, name):
        self.h = nc.alloc_semaphore(name)
        self.n = 0

    def inc(self, inst, k=1):
        inst.then_inc(self.h, k)
        self.n += k
        return self.n

    def wait(self, eng, n=None):
        eng.wait_ge(self.h, self.n if n is None else n)


class QSem:

    LANES = 8

    def __init__(self, nc, name):
        self.sems = [nc.alloc_semaphore(f"{name}{j}") for j in range(self.LANES)]
        self.cnt = [0] * self.LANES
        self.idx = 0

    def inc(self, inst, k=16):
        j = self.idx % self.LANES
        self.idx += 1
        inst.then_inc(self.sems[j], k)
        self.cnt[j] += k
        return (j, self.cnt[j])

    def wait(self, eng, tok):
        if isinstance(tok, list):
            for t in tok:
                self.wait(eng, t)
            return
        j, v = tok
        eng.wait_ge(self.sems[j], v)

    def snap(self):
        return [(j, self.cnt[j]) for j in range(self.LANES)]

    def wait_all(self, eng):
        for j, v in self.snap():
            if v:
                eng.wait_ge(self.sems[j], v)

    def state(self):
        return (self.idx, list(self.cnt))

    def restore(self, st):
        self.idx, cnt = st
        self.cnt = list(cnt)



def _host_prepare(inp):
    x = np.asarray(inp["L0_attn_x"], np.float32)
    ib = np.asarray(inp["L0_ffn_index_b"]).astype(np.int64)
    it = np.asarray(inp["L0_ffn_index_t"]).astype(np.int64)
    M = ib.shape[0]

    def T(a):
        return np.ascontiguousarray(np.asarray(a, np.float32).T)

    shared = dict(
        wqT=T(inp["L0_attn_Wq"]), wkT=T(inp["L0_attn_Wk"]), wvT=T(inp["L0_attn_Wv"]),
        sw1T=T(inp["L0_ffn_sw1"]), sw2T=T(inp["L0_ffn_sw2"]), sw3T=T(inp["L0_ffn_sw3"]),
        gwT=np.ascontiguousarray(np.concatenate(
            [np.asarray(inp["L0_ffn_sg_w"], np.float32),
             np.asarray(inp["L0_ffn_eg_w"], np.float32)], axis=0).T),
        ew1T=np.ascontiguousarray(np.asarray(inp["L0_ffn_ew1"], np.float32).transpose(0, 2, 1)),
        ew2T=np.ascontiguousarray(np.asarray(inp["L0_ffn_ew2"], np.float32).transpose(0, 2, 1)),
        ew3T=np.ascontiguousarray(np.asarray(inp["L0_ffn_ew3"], np.float32).transpose(0, 2, 1)),
    )

    def col4(v, n):
        return np.ascontiguousarray(np.asarray(v, np.float32).reshape(n, 128).T)

    cols = np.zeros((128, 88), np.float32)
    cols[:, 0:4] = col4(inp["L0_ffn_sb1"], 4)
    cols[:, 4:8] = col4(inp["L0_ffn_sb2"], 4)
    cols[:, 8:12] = col4(inp["L0_ffn_s_ln_g"], 4)
    cols[:, 12:16] = col4(inp["L0_ffn_s_ln_b"], 4)
    eb1 = np.asarray(inp["L0_ffn_eb1"], np.float32)
    eb2 = np.asarray(inp["L0_ffn_eb2"], np.float32)
    for e in range(NE):
        cols[:, 16 + 4 * e:20 + 4 * e] = col4(eb1[e], 4)
        cols[:, 48 + 4 * e:52 + 4 * e] = col4(eb2[e], 4)
    cols[:, 80:84] = col4(inp["L0_ffn_sel_ln_g"], 4)
    cols[:, 84:88] = col4(inp["L0_ffn_sel_ln_b"], 4)
    shared["cols"] = cols

    rows16 = np.zeros((16, D), np.float32)
    rows16[0] = np.asarray(inp["L0_ffn_sb3"], np.float32)
    rows16[1:9] = np.asarray(inp["L0_ffn_eb3"], np.float32)
    gb9 = np.concatenate([np.asarray(inp["L0_ffn_sg_b"], np.float32).reshape(-1),
                          np.asarray(inp["L0_ffn_eg_b"], np.float32).reshape(-1)])
    rows16[9, 0:9] = gb9
    shared["rows16"] = rows16
    shared["gb_rep"] = np.ascontiguousarray(
        np.broadcast_to(gb9[None, :], (128, 9)).astype(np.float32))
    shared["expb_rep"] = np.ascontiguousarray(np.broadcast_to(
        np.asarray(inp["L0_ffn_exp_bias"], np.float32).reshape(1, 8), (128, 8)))

    pp = np.arange(128)[:, None]
    cc = np.arange(128)[None, :]
    shared["trilM"] = (pp <= cc).astype(BF)
    shared["trilC"] = (np.float32(1e-9) * (pp > cc)).astype(BF)
    sel32 = np.zeros((32, 32 * 128), np.float32)
    for j in range(32):
        sel32[j, j * 128:(j + 1) * 128] = 1.0
    shared["sel32"] = sel32.astype(BF)
    sel128 = np.zeros((128, 32 * 32), np.float32)
    for i in range(32):
        sel128[:, i * 32 + (3 - i % 4) * 8 + i // 4] = 1.0
    shared["sel128"] = sel128.astype(BF)
    shared["eb3r"] = np.asarray(inp["L0_ffn_eb3"], np.float32)

    slots = [dict() for _ in range(NCORE)]
    for m in range(M):
        b0, t0 = int(ib[m]), int(it[m])
        stp = t0 // 128
        if stp >= 16 or b0 not in (0, 1) or (t0 % 2) != 0:
            raise ValueError("unsupported moe index pattern")
        c, l = _strip_owner(stp)
        key = (b0, 3 - l)
        slots[c].setdefault(key, {})
        if (t0 % 128) in slots[c][key]:
            raise ValueError("duplicate moe index")
        slots[c][key][t0 % 128] = m
    for c in range(NCORE):
        if sorted(slots[c].keys()) != sorted(TILE_ORDER):
            raise ValueError("unsupported moe index pattern (tiles)")
        for key in TILE_ORDER:
            if sorted(slots[c][key].keys()) != list(range(0, 128, 2)):
                raise ValueError("unsupported moe index pattern (rows)")

    in_maps = []
    for c in range(NCORE):
        SR = _strips_desc(c)
        xs = np.empty((4, B, 128, D), np.float32)
        for r in range(4):
            xs[r] = x[:, SR[r] * 128:(SR[r] + 1) * 128, :]
        xmoe = np.empty((256, D), np.float32)
        for k, (b0, r) in enumerate(TILE_ORDER):
            stp = SR[r]
            for j in range(64):
                xmoe[k * 64 + j] = x[b0, stp * 128 + 2 * j]
        uselp = np.zeros((32, 256), np.float32)
        for sp in range(32):
            co, lo = _strip_owner(sp)
            for b0 in range(B):
                for r in range(4):
                    if sp > SR[r]:
                        uselp[co * 4 + lo, b0 * 32 + (b0 * 4 + r)] = 1.0
        m = dict(shared)
        m.update(xs=xs, xmoe=xmoe, uselp=uselp)
        in_maps.append(m)
    return in_maps



class Ctx:
    pass


def _build(dbg=False):
    nc = bass.Bass("TRN2", target_bir_lowering=False)
    C = Ctx()
    C.nc = nc
    ik = "ExternalOutput" if dbg else "Internal"

    ti = {}
    for name, shape, dt in [
        ("xs", [4, B, 128, D], F32), ("xmoe", [256, D], F32), ("uselp", [32, 256], F32),
        ("wqT", [D, D], F32), ("wkT", [D, D], F32), ("wvT", [D, D], F32),
        ("sw1T", [D, HF], F32), ("sw2T", [D, HF], F32), ("sw3T", [HF, D], F32),
        ("gwT", [D, 9], F32),
        ("ew1T", [NE, D, HF], F32), ("ew2T", [NE, D, HF], F32), ("ew3T", [NE, HF, D], F32),
        ("cols", [128, 88], F32), ("rows16", [16, D], F32),
        ("gb_rep", [128, 9], F32), ("expb_rep", [128, 8], F32),
        ("trilM", [128, 128], BF16), ("trilC", [128, 128], BF16),
        ("sel32", [32, 32 * 128], BF16), ("sel128", [128, 32 * 32], BF16),
        ("eb3r", [8, D], F32),
    ]:
        ti[name] = nc.dram_tensor(name, shape, dt, kind="ExternalInput")
    C.ti = ti
    C.t_out = nc.dram_tensor("out", [4, B, 128, D], F32, kind="ExternalOutput")

    C.dbg = dbg
    C.i_kshard = nc.dram_tensor("kshard", [4 * B * 128, D], BF16, kind="Internal")
    C.i_vshard = nc.dram_tensor("vshard", [4 * B * 128, D], BF16, kind="Internal")
    C.i_sshard = nc.dram_tensor("sshard", [32, D], F32, kind="Internal")
    NKV = NCORE * 4 * B * 128
    C.i_gb = nc.dram_tensor("gblob", [2 * NKV, D], BF16, kind="Internal",
                            **(dict(addr_space="Shared") if USE_SHARED else {}))
    C.kg_off = 0
    C.vg_off = NKV * D
    C.i_sg = nc.dram_tensor("sgath", [NCORE * 32, D], F32, kind="Internal")
    C.i_fl = nc.dram_tensor("flagin", [1, 16], F32, kind="Internal")
    C.i_flo = nc.dram_tensor("flagout", [1, 16], F32, kind="Internal")
    C.i_xp = nc.dram_tensor("xpart", [4, B, 128, D], F32, kind=ik)
    C.i_qT = nc.dram_tensor("qTh", [B, 128, 8 * 4 * 128], BF16, kind=ik)
    C.i_E = nc.dram_tensor("Ebuf", [B, 128, 8448], BF16, kind=ik)
    if dbg:
        C.d_ks = nc.dram_tensor("d_kshard", [4 * B * 128, D], BF16, kind="ExternalOutput")
        C.d_vs = nc.dram_tensor("d_vshard", [4 * B * 128, D], BF16, kind="ExternalOutput")
        C.d_ss = nc.dram_tensor("d_sshard", [32, D], F32, kind="ExternalOutput")
        C.d_kg = nc.dram_tensor("d_kgath", [NCORE * 1024, D], BF16, kind="ExternalOutput")
        C.d_den = nc.dram_tensor("d_denom", [128, 8448], F32, kind="ExternalOutput")

    stk = ExitStack()
    C.stk = stk

    def sb(name, shape, dtype):
        return stk.enter_context(nc.sbuf_tensor(name, shape, dtype))

    C.s_cols = sb("s_cols", [128, 88], F32)
    C.s_agev = sb("s_agev", [128, 64], BF16)
    C.s_ages = sb("s_ages", [128, 16], F32)
    C.s_colsb = sb("s_colsb", [128, 88], BF16)
    C.s_rows = sb("s_rows", [16, D], F32)
    C.s_gbrep = sb("s_gbrep", [128, 9], F32)
    C.s_ebrep = sb("s_ebrep", [128, 8], F32)
    C.s_trilM = sb("s_trilM", [128, 128], BF16)
    C.s_trilC = sb("s_trilC", [128, 128], BF16)
    C.s_ones1 = sb("s_ones1", [1, 128], BF16)
    C.s_onesc = sb("s_onesc", [128, 1], BF16)
    C.s_BG = sb("s_BG", [2, D], F32)
    C.s_B1b = sb("s_B1b", [1, D], BF16)
    C.s_G1b = sb("s_G1b", [1, D], BF16)
    C.s_sel32 = sb("s_sel32", [32, 32 * 128], BF16)
    C.s_B1bc = sb("s_B1bc", [128, D], F32)
    C.s_sufb = sb("s_sufb", [32, D], BF16)
    C.s_uselp = sb("s_uselp", [32, 256], BF16)
    C.s_sel128 = sb("s_sel128", [128, 32 * 32], BF16)
    C.s_eb3r = sb("s_eb3r", [8, D], F32)
    C.s_xsumr = sb("s_xsumr", [32, D], BF16)
    C.s_xsumT = sb("s_xsumT", [128, 8 * 32], BF16)

    C.psum = stk.enter_context(nc.psum_tensor("ps", [128, 4096], F32))

    def pk(k, n=512, p=128):
        return C.psum[0:p, 512 * k: 512 * k + n]

    C.pk = pk

    C.sem = {}
    for n in ["ld", "sc", "gp", "st"]:
        C.sem[n] = QSem(nc, n)
    for n in ["te", "ve", "ac", "coll"]:
        C.sem[n] = SemT(nc, n)
    for s in C.sem.values():
        if isinstance(s, QSem):
            for h in s.sems:
                nc.gpsimd.sem_clear(h)
        else:
            nc.gpsimd.sem_clear(s.h)
    nc.all_engine_barrier()

    _emit_prologue(C)
    _emit_phase1(C)
    _emit_collectives(C)

    snap = {k: (s.state() if isinstance(s, QSem) else s.n)
            for k, s in C.sem.items()}
    engines = [nc.tensor, nc.scalar, nc.vector, nc.gpsimd, nc.sync]
    idx = [eng.compute_val(nc.partition_id([eng.engine])) for eng in engines]
    for c in nc.Switch(engines=engines, index=idx, n=NCORE):
        for k, s in C.sem.items():
            if isinstance(s, QSem):
                s.restore(snap[k])
            else:
                s.n = snap[k]
        _emit_2A(C, c)
        _emit_moe(C)
        _emit_2B(C, c)
    stk.close()
    return nc


def _emit_prologue(C):
    nc = C.nc
    ld, sc, gp, te, ve, ac = (C.sem[n] for n in ["ld", "sc", "gp", "te", "ve", "ac"])
    pk = C.pk
    ti = C.ti
    psum = C.psum

    for dst, src in [(C.s_cols, "cols"), (C.s_rows, "rows16"), (C.s_gbrep, "gb_rep"),
                     (C.s_ebrep, "expb_rep"), (C.s_trilM, "trilM"), (C.s_trilC, "trilC"),
                     (C.s_sel32, "sel32"), (C.s_sel128, "sel128"),
                     (C.s_eb3r, "eb3r")]:
        ld.inc(nc.sync.dma_start(out=dst[:], in_=ti[src].ap()))
    C.ld_consts = ld.snap()

    sb = lambda name, shape, dtype: C.stk.enter_context(nc.sbuf_tensor(name, shape, dtype))
    C.s_sw1 = sb("s_sw1", [128, 8 * HF], BF16)
    C.s_sw2 = sb("s_sw2", [128, 8 * HF], BF16)
    C.s_sw3 = sb("s_sw3", [128, 4 * D], BF16)
    C.s_gw = sb("s_gw", [128, 8 * 9], BF16)
    C.wscope = ExitStack()
    C.s_wq = C.wscope.enter_context(nc.sbuf_tensor("s_wq", [128, 8 * D], BF16))
    C.s_wk = C.wscope.enter_context(nc.sbuf_tensor("s_wk", [128, 8 * D], BF16))
    C.s_wv = C.wscope.enter_context(nc.sbuf_tensor("s_wv", [128, 8 * D], BF16))

    def wload(dst, src_t, ncol, nchunk, e=None):
        off = 0 if e is None else e * 128 * nchunk * ncol
        src = bass.AP(src_t, off, [[ncol, 128], [128 * ncol, nchunk], [1, ncol]])
        return gp.inc(nc.gpsimd.dma_start(out=dst[:], in_=src))

    C.wload = wload
    wload(C.s_wq, ti["wqT"], D, 8)
    wload(C.s_wk, ti["wkT"], D, 8)
    wload(C.s_wv, ti["wvT"], D, 8)
    wload(C.s_sw1, ti["sw1T"], HF, 8)
    wload(C.s_sw2, ti["sw2T"], HF, 8)
    wload(C.s_sw3, ti["sw3T"], D, 4)
    wload(C.s_gw, ti["gwT"], 9, 8)
    gp.inc(nc.gpsimd.dma_start(out=C.s_uselp[:], in_=ti["uselp"].ap()))
    C.gp_weights = gp.snap()

    ld.wait(nc.vector, C.ld_consts)
    nc.vector.memset(C.s_ones1[:], 1.0)
    nc.vector.memset(C.s_onesc[:], 1.0)
    ve.inc(nc.vector.tensor_copy(out=C.s_colsb[:], in_=C.s_cols[:]))
    C.ve_consts = ve.n

    gp.wait(nc.scalar, C.gp_weights)
    ld.wait(nc.scalar, C.ld_consts)
    relay = _age_sc(C)
    ac.wait(nc.tensor, relay)
    gp.wait(nc.tensor, C.gp_weights)
    ve.wait(nc.tensor, C.ve_consts)
    for f in range(4):
        for h in range(2):
            nc.tensor.matmul(pk(0 + h, 512, 1), C.s_colsb[:, 12 + f:13 + f],
                             C.s_sw3[:, f * D + h * 512: f * D + h * 512 + 512],
                             start=(f == 0), stop=(f == 3))
            mm = nc.tensor.matmul(pk(2 + h, 512, 1), C.s_colsb[:, 8 + f:9 + f],
                                  C.s_sw3[:, f * D + h * 512: f * D + h * 512 + 512],
                                  start=(f == 0), stop=(f == 3))
    te.inc(mm)
    te.wait(nc.vector)
    nc.vector.tensor_tensor(out=C.s_BG[0:1, 0:512], in0=psum[0:1, 0:512],
                            in1=C.s_rows[0:1, 0:512], op=ALU.add)
    nc.vector.tensor_tensor(out=C.s_BG[0:1, 512:1024], in0=psum[0:1, 512:1024],
                            in1=C.s_rows[0:1, 512:1024], op=ALU.add)
    nc.vector.tensor_copy(out=C.s_G1b[0:1, 0:512], in_=psum[0:1, 1024:1536])
    nc.vector.tensor_copy(out=C.s_G1b[0:1, 512:1024], in_=psum[0:1, 1536:2048])
    ve.inc(nc.vector.tensor_copy(out=C.s_B1b[:], in_=C.s_BG[0:1, :]))
    ve.wait(nc.tensor)
    nc.tensor.matmul(pk(4), C.s_ones1[0:1, :], C.s_B1b[0:1, 0:512], start=True, stop=True)
    te.inc(nc.tensor.matmul(pk(5), C.s_ones1[0:1, :], C.s_B1b[0:1, 512:1024],
                            start=True, stop=True))
    te.wait(nc.vector)
    nc.vector.tensor_copy(out=C.s_B1bc[:, 0:512], in_=pk(4))
    ve.inc(nc.vector.tensor_copy(out=C.s_B1bc[:, 512:1024], in_=pk(5)))
    C.ve_b1bc = ve.n


def _emit_phase1(C):
    nc = C.nc
    ld, sc, gp, te, ve, ac = (C.sem[n] for n in ["ld", "sc", "gp", "te", "ve", "ac"])
    pk = C.pk
    psum = C.psum

    p1 = ExitStack()

    def sp(name, shape, dtype):
        return p1.enter_context(nc.sbuf_tensor(name, shape, dtype))

    s_xt = [sp(f"s_xt{i}", [128, D], F32) for i in range(4)]
    s_xnb = [sp(f"s_xnb{i}", [128, D], BF16) for i in range(4)]
    s_xnT = [sp(f"s_xnT{i}", [128, 4096], BF16) for i in range(2)]
    s_scr = sp("s_scr", [128, D], BF16)
    s_st4 = [sp(f"s_st4{i}", [128, 36], F32) for i in range(2)]
    s_silu = sp("s_silu", [128, HF], F32)
    s_ygs = sp("s_ygs", [128, 4 * HF], BF16)
    s_ybf = sp("s_ybf", [128, HF], BF16)
    s_y2 = sp("s_y2", [128, HF], BF16)
    s_smu = sp("s_smu", [1, HF], F32)
    s_sex2 = sp("s_sex2", [1, HF], F32)
    s_stmp = sp("s_stmp", [1, HF], F32)
    s_srin = sp("s_srin", [1, HF], F32)
    s_srst = sp("s_srst", [1, HF], F32)
    s_nmu = sp("s_nmu", [1, HF], BF16)
    s_rpk = sp("s_rpk", [16, HF], BF16)
    s_rcol = [sp(f"s_rcol{i}", [128, 16], BF16) for i in range(2)]
    s_rcf = [sp(f"s_rcf{i}", [128, 1], F32) for i in range(2)]
    s_qst = sp("s_qst", [128, 4096], BF16)
    s_dmy = sp("s_dmy", [128, 512], BF16)
    s_kst = [sp(f"s_kst{i}", [128, D], BF16) for i in range(2)]
    s_vst = [sp(f"s_vst{i}", [128, D], BF16) for i in range(2)]
    s_xps = [sp(f"s_xps{i}", [128, D], F32) for i in range(2)]
    s_sx = sp("s_sx", [32, D], F32)

    ap_xs = C.ti["xs"].ap()

    xt_ld = {}
    xn_ve = {}
    xnb_tr = {}
    xnT_tok = {}
    xsum_te = {}
    p1_ac, p2_ve, p3_ac = {}, {}, {}
    xnT_te = {}
    qst_gp = {}
    kv_gp = {}
    xp_gp = {}
    xp_ve = {}

    ve.wait(nc.scalar, C.ve_b1bc)
    ld.wait(nc.scalar, C.ld_consts)
    ve.wait(nc.tensor, C.ve_b1bc)

    def emit_loads(b):
        for r in range(4):
            if b >= 2:
                ve.wait(nc.sync, xn_ve[(b - 1, r)])
            xt_ld[(b, r)] = ld.inc(nc.sync.dma_start(out=s_xt[r][:], in_=ap_xs[r, b]))

    def emit_p1(b):
        sl = b % 2
        st4 = s_st4[sl]
        if b >= 2:
            ve.wait(nc.scalar, xn_ve[(b - 2, 3)])
        for r in range(4):
            ld.wait(nc.scalar, xt_ld[(b, r)])
            _age_sc(C)
            nc.scalar.activation(out=s_scr[:], in_=s_xt[r][:], func=AF.Copy,
                                 accum_out=st4[:, r: r + 1])
            a = nc.scalar.activation(out=s_scr[:], in_=s_xt[r][:], func=AF.Square,
                                     accum_out=st4[:, 4 + r: 5 + r])
        ac.inc(a)
        p1_ac[b] = ac.n

    def emit_p2(b):
        sl = b % 2
        st4 = s_st4[sl]
        ac.wait(nc.vector, p1_ac[b])
        _vh(C, nc.vector.tensor_scalar(out=st4[:, 8:16], in0=st4[:, 0:8],
                                       scalar1=1.0 / D, scalar2=None, op0=ALU.mult))
        _vh(C, nc.vector.tensor_tensor(out=st4[:, 16:20], in0=st4[:, 8:12],
                                       in1=st4[:, 8:12], op=ALU.mult))
        _vh(C, nc.vector.tensor_tensor(out=st4[:, 20:24], in0=st4[:, 12:16],
                                       in1=st4[:, 16:20], op=ALU.subtract))
        _vh(C, nc.vector.tensor_scalar(out=st4[:, 24:28], in0=st4[:, 20:24],
                                       scalar1=1e-5, scalar2=None, op0=ALU.add))
        ve.inc(nc.vector.reciprocal(out=st4[:, 28:32], in_=st4[:, 24:28]))
        p2_ve[b] = ve.n

    def emit_p3(b):
        sl = b % 2
        st4 = s_st4[sl]
        ve.wait(nc.scalar, p2_ve[b])
        ac.inc(nc.scalar.activation(out=st4[:, 32:36], in_=st4[:, 28:32], func=AF.Sqrt))
        p3_ac[b] = ac.n

    def emit_xn(b):
        sl = b % 2
        st4 = s_st4[sl]
        ac.wait(nc.vector, p3_ac[b])
        if b >= 2:
            te.wait(nc.scalar, xnT_te[b - 2])
            te.wait(nc.sync, xnT_te[b - 2])
        for r in range(4):
            ld.wait(nc.vector, xt_ld[(b, r)])
            if b >= 1:
                te.wait(nc.vector, xsum_te[(b - 1, r)])
                for q in xnb_tr[(b - 1, r)]:
                    q[0].wait(nc.vector, q[1])
            ve.inc(nc.vector.tensor_scalar(out=s_xnb[r][:], in0=s_xt[r][:],
                                           scalar1=st4[:, 8 + r: 9 + r],
                                           scalar2=st4[:, 32 + r: 33 + r],
                                           op0=ALU.subtract, op1=ALU.mult))
            xn_ve[(b, r)] = ve.n
        ve.inc(nc.vector.memset(s_dmy[:], 0.0))
        xn_aged = ve.n
        ve.wait(nc.scalar, xn_aged)
        ve.wait(nc.sync, xn_aged)
        toks = []
        for r in range(4):
            rtoks = []
            for h in range(8):
                eng, qs = (nc.scalar, sc) if h % 2 == 0 else (nc.sync, ld)
                rtoks.append((qs, qs.inc(eng.dma_start(
                    out=s_xnT[sl][:, h * 512 + r * 128: h * 512 + r * 128 + 128],
                    in_=s_xnb[r][:, h * 128: h * 128 + 128], transpose=True))))
            xnb_tr[(b, r)] = rtoks
            toks.extend(rtoks)
        xnT_tok[b] = toks

    emit_loads(0)
    emit_p1(0)
    emit_p2(0)
    emit_p3(0)
    emit_xn(0)
    emit_loads(1)
    emit_p1(1)

    for b in range(B):
        bsl = b % 2

        for r in range(4):
            i = b * 4 + r
            ve.wait(nc.tensor, xn_ve[(b, r)])
            nc.tensor.matmul(pk(6, 512, 32), C.s_sel128[:, i * 32: i * 32 + 32],
                             s_xnb[r][:, 0:512],
                             start=(i == 0), stop=(i == 31), skip_group_check=True)
            te.inc(nc.tensor.matmul(pk(7, 512, 32), C.s_sel128[:, i * 32: i * 32 + 32],
                                    s_xnb[r][:, 512:1024],
                                    start=(i == 0), stop=(i == 31),
                                    skip_group_check=True))
            xsum_te[(b, r)] = te.n

        for q in xnT_tok[b]:
            q[0].wait(nc.tensor, q[1])
        if b >= 1:
            gp.wait(nc.vector, qst_gp[b - 1])
        qev = {}
        for q in range(8):
            bank = q % 2
            if q >= 2:
                ve.wait(nc.tensor, qev[q - 2])
            for h in range(8):
                mm = nc.tensor.matmul(
                    pk(bank), C.s_wq[:, h * D + q * 128: h * D + q * 128 + 128],
                    s_xnT[bsl][:, h * 512: h * 512 + 512],
                    start=(h == 0), stop=(h == 7))
            te.inc(mm)
            te.wait(nc.vector)
            ve.inc(nc.vector.tensor_copy(out=s_qst[:, q * 512: q * 512 + 512],
                                         in_=pk(bank)))
            qev[q] = ve.n
        qT_evacs_ve = ve.n
        ve.wait(nc.gpsimd)
        qst_gp[b] = gp.inc(nc.gpsimd.dma_start(out=C.i_qT.ap()[b], in_=s_qst[:]))

        if b + 1 < B:
            emit_p2(b + 1)

        kv_ac = 0
        for r in range(4):
            ksl = r % 2
            if r >= 2:
                gp.wait(nc.scalar, kv_gp[(b, r - 2)])
            elif b >= 1:
                gp.wait(nc.scalar, kv_gp[(b - 1, r + 2)])
            if kv_ac:
                ac.wait(nc.tensor, kv_ac)
            for h in range(8):
                lhs = s_xnT[bsl][:, h * 512 + r * 128: h * 512 + r * 128 + 128]
                nc.tensor.matmul(pk(2), lhs, C.s_wk[:, h * D: h * D + 512],
                                 start=(h == 0), stop=(h == 7))
                nc.tensor.matmul(pk(3), lhs, C.s_wk[:, h * D + 512: h * D + 1024],
                                 start=(h == 0), stop=(h == 7))
                nc.tensor.matmul(pk(4), lhs, C.s_wv[:, h * D: h * D + 512],
                                 start=(h == 0), stop=(h == 7))
                mm = nc.tensor.matmul(pk(5), lhs, C.s_wv[:, h * D + 512: h * D + 1024],
                                      start=(h == 0), stop=(h == 7))
            te.inc(mm)
            te.wait(nc.scalar)
            nc.scalar.activation(out=s_kst[ksl][:], in_=psum[:, 1024:2048], func=AF.Copy)
            ac.inc(nc.scalar.activation(out=s_vst[ksl][:], in_=psum[:, 2048:3072],
                                        func=AF.Copy))
            kv_ac = ac.n
            ac.wait(nc.gpsimd)
            l = 3 - r
            t1 = gp.inc(nc.gpsimd.dma_start(
                out=bass.AP(C.i_kshard, (l * B + b) * 128 * D, [[D, 128], [1, D]]),
                in_=s_kst[ksl][:]))
            t2 = gp.inc(nc.gpsimd.dma_start(
                out=bass.AP(C.i_vshard, (l * B + b) * 128 * D, [[D, 128], [1, D]]),
                in_=s_vst[ksl][:]))
            kv_gp[(b, r)] = [t1, t2]

        if b + 1 < B:
            emit_p3(b + 1)
            emit_xn(b + 1)

        ac.wait(nc.tensor, kv_ac)
        for f in range(4):
            if f == 0:
                ve.wait(nc.tensor, qT_evacs_ve)
            if f > 0:
                ve.wait(nc.tensor)
                ac.wait(nc.tensor)
            for h in range(8):
                rhs = s_xnT[bsl][:, h * 512: h * 512 + 512]
                nc.tensor.matmul(pk(2), C.s_sw1[:, h * HF + f * 128: h * HF + f * 128 + 128],
                                 rhs, start=(h == 0), stop=(h == 7))
                mm = nc.tensor.matmul(pk(3), C.s_sw2[:, h * HF + f * 128: h * HF + f * 128 + 128],
                                      rhs, start=(h == 0), stop=(h == 7))
            te.inc(mm)
            te.wait(nc.scalar)
            ac.inc(nc.scalar.activation(out=s_silu[:], in_=pk(2), func=AF.Silu,
                                        bias=C.s_cols[:, 0 + f:1 + f]))
            ac.wait(nc.vector)
            nc.vector.scalar_tensor_tensor(out=s_ybf[:], in0=pk(3),
                                           scalar=C.s_cols[:, 4 + f:5 + f], in1=s_silu[:],
                                           op0=ALU.add, op1=ALU.mult)
            ve.inc(nc.vector.tensor_scalar(out=s_ygs[:, f * HF: f * HF + HF],
                                           in0=s_ybf[:],
                                           scalar1=C.s_cols[:, 8 + f:9 + f],
                                           scalar2=None, op0=ALU.mult))
            ve.wait(nc.scalar)
            ac.inc(nc.scalar.activation(out=s_y2[:], in_=s_ybf[:], func=AF.Square))
            ac.wait(nc.tensor)
            nc.tensor.matmul(pk(0, 512, 1), C.s_onesc[:, 0:1], s_ybf[:],
                             start=(f == 0), stop=(f == 3))
            mm = nc.tensor.matmul(pk(1, 512, 1), C.s_onesc[:, 0:1], s_y2[:],
                                  start=(f == 0), stop=(f == 3))
            te.inc(mm)
            te.wait(nc.vector)
        nc.vector.tensor_scalar(out=s_smu[:], in0=psum[0:1, 0:512],
                                scalar1=1.0 / HF, scalar2=None, op0=ALU.mult)
        nc.vector.tensor_scalar(out=s_sex2[:], in0=psum[0:1, 512:1024],
                                scalar1=1.0 / HF, scalar2=None, op0=ALU.mult)
        nc.vector.tensor_tensor(out=s_stmp[:], in0=s_smu[:], in1=s_smu[:], op=ALU.mult)
        nc.vector.tensor_tensor(out=s_stmp[:], in0=s_sex2[:], in1=s_stmp[:],
                                op=ALU.subtract)
        nc.vector.tensor_scalar(out=s_stmp[:], in0=s_stmp[:],
                                scalar1=1e-5, scalar2=None, op0=ALU.add)
        ve.inc(nc.vector.reciprocal(out=s_srin[:], in_=s_stmp[:]))
        stats_ev = ve.n
        ve.wait(nc.scalar, stats_ev)
        nc.scalar.activation(out=s_srst[:], in_=s_srin[:], func=AF.Sqrt)
        nc.scalar.activation(out=s_nmu[:], in_=s_smu[:], func=AF.Copy, scale=-1.0)
        ac.inc(nc.scalar.activation(out=s_rpk[0:1, :], in_=s_srst[:], func=AF.Copy))
        rows_ac = ac.n
        for t in range(4):
            gi = b * 4 + t
            if gi >= 2:
                ve.wait(nc.scalar, xp_ve[gi - 2])
            ac.wait(nc.scalar, rows_ac)
            rc_sc = sc.inc(nc.scalar.dma_start(out=s_rcol[t % 2][:],
                                               in_=s_rpk[:, t * 128: t * 128 + 128],
                                               transpose=True))
            if t == 0:
                ve.wait(nc.tensor, stats_ev)
                ac.wait(nc.tensor, rows_ac)
            else:
                ve.wait(nc.tensor, xp_ve[gi - 1])
            for f in range(4):
                nc.tensor.matmul(pk(2), s_ygs[:, f * HF + t * 128: f * HF + t * 128 + 128],
                                 C.s_sw3[:, f * D: f * D + 512],
                                 start=(f == 0), stop=False)
                nc.tensor.matmul(pk(3), s_ygs[:, f * HF + t * 128: f * HF + t * 128 + 128],
                                 C.s_sw3[:, f * D + 512: f * D + 1024],
                                 start=(f == 0), stop=False)
            nc.tensor.matmul(pk(2), s_nmu[0:1, t * 128: t * 128 + 128],
                             C.s_G1b[0:1, 0:512], start=False, stop=True)
            mm = nc.tensor.matmul(pk(3), s_nmu[0:1, t * 128: t * 128 + 128],
                                  C.s_G1b[0:1, 512:1024], start=False, stop=True)
            te.inc(mm)
            a_te = te.n
            if gi >= 2:
                gp.wait(nc.sync, xp_gp[gi - 2])
            xp_ld = ld.inc(nc.sync.dma_start(out=s_xps[t % 2][:], in_=ap_xs[t, b]))
            te.wait(nc.vector, a_te)
            sc.wait(nc.vector, rc_sc)
            ld.wait(nc.vector, xp_ld)
            _vh(C, nc.vector.tensor_copy(out=s_rcf[t % 2][:], in_=s_rcol[t % 2][:, 0:1]))
            nc.vector.tensor_tensor(out=s_xps[t % 2][:], in0=s_xps[t % 2][:],
                                    in1=C.s_B1bc[:], op=ALU.add)
            nc.vector.scalar_tensor_tensor(out=s_xps[t % 2][:],
                                           in0=psum[:, 1024:2048],
                                           scalar=s_rcf[t % 2][:, 0:1],
                                           in1=s_xps[t % 2][:],
                                           op0=ALU.mult, op1=ALU.add)
            ve.inc(nc.vector.memset(s_dmy[:, 0:64], 0.0))
            xp_ve[gi] = ve.n
            ve.wait(nc.gpsimd)
            xp_gp[gi] = gp.inc(nc.gpsimd.dma_start(out=C.i_xp.ap()[t, b],
                                                   in_=s_xps[t % 2][:]))
            ve.wait(nc.tensor, xp_ve[gi])
        xnT_te[b] = te.n

        if b + 2 < B:
            emit_loads(b + 2)
            emit_p1(b + 2)

    te.wait(nc.vector)
    _age_ve(C)
    nc.vector.tensor_copy(out=C.s_xsumr[:], in_=psum[0:32, 3072:4096])
    ve.inc(nc.vector.memset(s_dmy[:], 0.0))
    ve.wait(nc.scalar)
    xsT_toks = []
    for h in range(8):
        xsT_toks.append(sc.inc(nc.scalar.dma_start(
            out=C.s_xsumT[:, h * 32: h * 32 + 32],
            in_=C.s_xsumr[:, h * 128: h * 128 + 128], transpose=True)))
    sc.wait(nc.tensor, xsT_toks)
    ve.wait(nc.tensor)
    for h in range(8):
        nc.tensor.matmul(pk(0, 512, 32), C.s_xsumT[:, h * 32: h * 32 + 32],
                         C.s_wv[:, h * D: h * D + 512], start=(h == 0), stop=(h == 7))
        mm = nc.tensor.matmul(pk(1, 512, 32), C.s_xsumT[:, h * 32: h * 32 + 32],
                              C.s_wv[:, h * D + 512: h * D + 1024],
                              start=(h == 0), stop=(h == 7))
    te.inc(mm)
    te.wait(nc.vector)
    nc.vector.tensor_copy(out=s_sx[:], in_=psum[0:32, 0:1024])
    ve.inc(nc.vector.memset(s_dmy[:, 0:64], 0.0))
    ve.wait(nc.gpsimd)
    gp.inc(nc.gpsimd.dma_start(out=C.i_sshard.ap(), in_=s_sx[:]))
    C.p1_gp = gp.snap()
    C.p1_te = te.n
    C.p1_ve = ve.n
    p1.close()
    C.wscope.close()


def _emit_collectives(C):
    nc = C.nc
    gp, coll = C.sem["gp"], C.sem["coll"]
    rg = [list(range(NCORE))]
    gp.wait(nc.gpsimd, C.p1_gp)
    coll.inc(nc.gpsimd.collective_compute("AllGather", ALU.bypass,
                                          ins=[C.i_kshard.ap()],
                                          outs=[bass.AP(C.i_gb, C.kg_off,
                                                        [[D, NCORE * 4 * B * 128], [1, D]])],
                                          replica_groups=rg))
    coll.inc(nc.gpsimd.collective_compute("AllReduce", ALU.add,
                                          ins=[C.i_fl.ap()], outs=[C.i_flo.ap()],
                                          replica_groups=rg))
    coll.inc(nc.gpsimd.collective_compute("AllGather", ALU.bypass,
                                          ins=[C.i_vshard.ap()],
                                          outs=[bass.AP(C.i_gb, C.vg_off,
                                                        [[D, NCORE * 4 * B * 128], [1, D]])],
                                          replica_groups=rg))
    coll.inc(nc.gpsimd.collective_compute("AllGather", ALU.bypass,
                                          ins=[C.i_sshard.ap()], outs=[C.i_sg.ap()],
                                          replica_groups=rg))
    coll.inc(nc.gpsimd.collective_compute("AllReduce", ALU.add,
                                          ins=[C.i_fl.ap()], outs=[C.i_flo.ap()],
                                          replica_groups=rg))
    C.coll_k, C.coll_v, C.coll_s = 2, 5, 5

    p2a = ExitStack()

    def sa(name, shape, dtype):
        return p2a.enter_context(nc.sbuf_tensor(name, shape, dtype))

    C.s_denom = sa("s_denom", [128, 8448], F32)
    C.sem["ve"].inc(nc.vector.memset(C.s_denom[:], 0.0))
    C.s_ew1 = [sa(f"s_ew1{i}", [128, 8 * HF], BF16) for i in range(2)]
    C.s_ew2 = [sa(f"s_ew2{i}", [128, 8 * HF], BF16) for i in range(2)]
    C.s_ew3 = [sa(f"s_ew3{i}", [128, 4 * D], BF16) for i in range(2)]

    C.ew_ld0 = []
    for e in range(2):
        C.ew_ld0.append(C.wload(C.s_ew1[e], C.ti["ew1T"], HF, 8, e))
        C.ew_ld0.append(C.wload(C.s_ew2[e], C.ti["ew2T"], HF, 8, e))
        C.ew_ld0.append(C.wload(C.s_ew3[e], C.ti["ew3T"], D, 4, e))

    p2b = ExitStack()
    C.s_ka = [p2b.enter_context(nc.sbuf_tensor(f"s_ka{i}", [128, 8192], BF16))
              for i in range(2)]
    C.s_est = [p2b.enter_context(nc.sbuf_tensor(f"s_est{i}", [128, 4096], BF16))
               for i in range(2)]
    C.s_qtb = [p2b.enter_context(nc.sbuf_tensor(f"s_qtb{i}", [128, 4096], BF16))
               for i in range(2)]
    p2b.close()

    p2c = ExitStack()

    def sm(name, shape, dtype):
        return p2c.enter_context(nc.sbuf_tensor(name, shape, dtype))

    C.s_xmt = [sm(f"s_xmt{i}", [128, D], F32) for i in range(2)]
    C.s_xmb = sm("s_xmb", [128, D], BF16)
    C.s_xmT = sm("s_xmT", [128, 8 * 256], BF16)
    C.s_mscr = sm("s_mscr", [128, D], BF16)
    C.s_mst = sm("s_mst", [128, 10], F32)
    C.s_g9 = [sm(f"s_g9{i}", [128, 9], F32) for i in range(2)]
    C.s_e8 = sm("s_e8", [128, 8], F32)
    C.s_m8 = sm("s_m8", [128, 8], F32)
    C.s_oh = [sm(f"s_oh{i}", [128, 8], F32) for i in range(2)]
    C.s_w2 = sm("s_w2", [128, 2], F32)
    C.s_w2e = sm("s_w2e", [128, 2], F32)
    C.s_wex = [sm(f"s_wex{i}", [128, 1], F32) for i in range(2)]
    C.s_wshm = [sm(f"s_wshm{i}", [128, 1], F32) for i in range(2)]
    C.s_msil = sm("s_msil", [128, 256], F32)
    C.s_mybf = sm("s_mybf", [128, 256], BF16)
    C.s_my2 = sm("s_my2", [128, 256], BF16)
    C.s_mygs = sm("s_mygs", [128, 4 * 256], BF16)
    C.s_mmu = sm("s_mmu", [1, 256], F32)
    C.s_mex2 = sm("s_mex2", [1, 256], F32)
    C.s_mtmp = sm("s_mtmp", [1, 256], F32)
    C.s_mrin = sm("s_mrin", [1, 256], F32)
    C.s_mrst = sm("s_mrst", [1, 256], F32)
    C.s_selbE = sm("s_selbE", [128, 32], BF16)
    C.s_mnmu = sm("s_mnmu", [1, 256], BF16)
    C.s_mrpk = sm("s_mrpk", [16, 256], BF16)
    C.s_mrc = [sm(f"s_mrc{i}", [128, 16], BF16) for i in range(2)]
    C.s_mrcf = [sm(f"s_mrcf{i}", [128, 1], F32) for i in range(2)]
    C.s_sm = [sm(f"s_sm{i}", [128, D], F32) for i in range(2)]
    C.s_moeA = [sm(f"s_moeA{i}", [128, D], F32) for i in range(2)]
    C.s_acc = [sm(f"s_acc{i}", [128, D], F32) for i in range(2)]
    C.s_b1e = sm("s_b1e", [8, D], BF16)
    C.s_g1e = sm("s_g1e", [1, D], BF16)
    C.s_ohew = [sm(f"s_ohew{i}", [128, 8], F32) for i in range(2)]
    C.s_pk128 = sm("s_pk128", [128, 128], BF16)
    C.s_pkT = [sm(f"s_pkT{i}", [128, 128], BF16) for i in range(2)]
    C.s_sc1 = sm("s_sc1", [128, 1], F32)
    p2c.close()

    p2d = ExitStack()

    def sd(name, shape, dtype):
        return p2d.enter_context(nc.sbuf_tensor(name, shape, dtype))

    C.s_erd = [sd(f"s_erd{i}", [128, 4096], BF16) for i in range(2)]
    C.s_vr = [sd(f"s_vr{i}", [128, D], BF16) for i in range(4)]
    C.s_att = [sd(f"s_att{i}", [128, 512], BF16) for i in range(2)]
    C.s_x2p = [sd(f"s_x2p{i}", [128, D], F32) for i in range(2)]
    C.s_ost = [sd(f"s_ost{i}", [128, D], F32) for i in range(2)]
    C.s_sb2 = sd("s_sb2", [32, D], BF16)
    p2d.close()
    p2a.close()


def _age_ve(C):
    C.sem["ve"].inc(C.nc.vector.memset(C.s_agev[:], 0.0))
    return C.sem["ve"].n


def _age_te(C):
    return C.nc.tensor.matmul(C.psum[64:65, 2048:2080], C.s_sel128[:, 0:1],
                              C.s_sel128[:, 0:32], start=True, stop=True,
                              skip_group_check=True)


def _age_sc(C):
    C.sem["ac"].inc(C.nc.scalar.activation(out=C.s_ages[:], in_=C.s_cols[:, 0:16],
                                           func=AF.Copy))
    return C.sem["ac"].n


def _vh(C, inst):
    ve = C.sem["ve"]
    ve.inc(inst)
    ve.wait(C.nc.vector)


def _cum_of(c):
    SR = _strips_desc(c)
    umax = SR[0]

    def nb_of(u):
        return sum(1 for s in SR if s >= u)

    cum = [0]
    for u in range(32):
        cum.append(cum[-1] + (nb_of(u) * 128 if u <= umax else 0))
    return SR, umax, nb_of, cum


def _emit_2A(C, c):
    nc = C.nc
    ld, sc, gp, te, ve, ac = (C.sem[n] for n in ["ld", "sc", "gp", "te", "ve", "ac"])
    pk = C.pk
    SR, umax, nb_of, cum = _cum_of(c)
    amax = umax // 8
    coll = C.sem["coll"]

    coll.wait(nc.sync, C.coll_k)
    ve.wait(nc.tensor, C.p1_ve)
    if C.dbg:
        coll.wait(nc.gpsimd, C.coll_k)
        st = C.sem["st"]
        for cc in range(NCORE):
            st.inc(nc.gpsimd.dma_start(
                out=bass.AP(C.d_kg, cc * 1024 * D, [[D, 1024], [1, D]]),
                in_=bass.AP(C.i_gb, C.kg_off + cc * 4096 * D, [[D, 1024], [1, D]])))
    s_ka, s_est, s_qtb = C.s_ka, C.s_est, C.s_qtb
    qtb_ld, qtb_te, ka_ld, ka_te, est_sc = {}, {}, {}, {}, {}
    bank_ac = {}
    es = D
    bs = 8 * 128 * es
    for b in range(B):
        if b >= 2:
            te.wait(nc.sync, qtb_te[b - 2])
        qtb_ld[b] = ld.inc(nc.sync.dma_start(out=s_qtb[b % 2][:], in_=C.i_qT.ap()[b]))
        for a in range(amax + 1):
            g = b * (amax + 1) + a
            sl = g % 2
            if g >= 2:
                te.wait(nc.sync, ka_te[g - 2])
            katoks = []
            for part0, base_blk, qsgn in ((0, 0, 1), (64, 4, 1), (0, 30, -1), (64, 26, -1)):
                qstride = qsgn * 8 * bs
                colbase = 0 if qsgn == 1 else 4096
                for p2 in range(2):
                    off = ((base_blk + p2) * 8 + b) * 128 * es + a * es
                    dst = s_ka[sl][part0 + 32 * p2: part0 + 32 * p2 + 32,
                                   colbase:colbase + 4096]
                    src = bass.AP(C.i_gb, C.kg_off + off,
                                  [[4 * es, 32], [qstride, 4], [1, es]])
                    katoks.append(ld.inc(nc.sync.dma_start(
                        out=dst.rearrange("p (q n) -> p q n", q=4), in_=src)))
            ka_ld[g] = katoks
            ld.wait(nc.scalar, katoks)
            if a == 0:
                ld.wait(nc.scalar, qtb_ld[b])
            ka_relay = _age_sc(C)
            u_hi = min(8 * a + 7, umax)
            mm_te = {}

            def emit_exp(u2):
                nb2 = nb_of(u2)
                bank2 = u2 % 4
                off2 = cum[u2] - cum[8 * a]
                te.wait(nc.scalar, mm_te[u2])
                _age_sc(C)
                ac.inc(nc.scalar.activation(
                    out=s_est[sl][:, off2: off2 + nb2 * 128],
                    in_=pk(bank2, nb2 * 128), func=AF.Exp, scale=1.0 / 32.0))
                bank_ac[bank2] = ac.n
                ac.wait(nc.vector)
                ve.inc(nc.vector.tensor_tensor(
                    out=C.s_denom[:, cum[u2]: cum[u2] + nb2 * 128],
                    in0=C.s_denom[:, cum[u2]: cum[u2] + nb2 * 128],
                    in1=s_est[sl][:, off2: off2 + nb2 * 128], op=ALU.add))

            pend = None
            for u in range(8 * a, u_hi + 1):
                nb = nb_of(u)
                bank = u % 4
                if bank in bank_ac:
                    ac.wait(nc.tensor, bank_ac[bank])
                if u == 8 * a:
                    ld.wait(nc.tensor, ka_ld[g])
                    ld.wait(nc.tensor, qtb_ld[b])
                    ac.wait(nc.tensor, ka_relay)
                    if g >= 2:
                        sc.wait(nc.scalar, est_sc[g - 2])
                for q in range(8):
                    mm = nc.tensor.matmul(
                        pk(bank, nb * 128),
                        s_ka[sl][:, q * 1024 + (u % 8) * 128: q * 1024 + (u % 8) * 128 + 128],
                        s_qtb[b % 2][:, q * 512: q * 512 + nb * 128],
                        start=(q == 0), stop=(q == 7))
                if u <= 8 * a + 2:
                    mm = _age_te(C)
                te.inc(mm)
                mm_te[u] = te.n
                if pend is not None:
                    emit_exp(pend)
                pend = u
            emit_exp(pend)
            ka_te[g] = te.n
            W = cum[u_hi + 1] - cum[8 * a]
            ve.wait(nc.scalar)
            est_sc[g] = sc.inc(nc.scalar.dma_start(
                out=C.i_E.ap()[b][:, cum[8 * a]: cum[8 * a] + W],
                in_=s_est[sl][:, 0:W]))
        qtb_te[b] = te.n
    C.a2_sc = sc.snap()
    C.a2_te = te.n
    C.a2_ve = ve.n
    C.a2_ac = ac.n


def _emit_moe(C):
    nc = C.nc
    ld, sc, gp, te, ve, ac = (C.sem[n] for n in ["ld", "sc", "gp", "te", "ve", "ac"])
    pk = C.pk
    psum = C.psum
    ap_xm = C.ti["xmoe"].ap()

    te.wait(nc.sync, C.a2_te)
    sc.wait(nc.sync, C.a2_sc)
    te.wait(nc.scalar, C.a2_te)
    sc.wait(nc.scalar, C.a2_sc)
    te.wait(nc.vector, C.a2_te)
    sc.wait(nc.vector, C.a2_sc)
    te.wait(nc.gpsimd, C.a2_te)
    sc.wait(nc.gpsimd, C.a2_sc)

    xmT_sc = []
    for t in range(2):
        xmtok = ld.inc(nc.sync.dma_start(out=C.s_xmt[t][:],
                                         in_=ap_xm[t * 128:(t + 1) * 128, :]))
        ld.wait(nc.scalar, xmtok)
        _age_sc(C)
        if t == 1:
            ve.wait(nc.scalar)
        nc.scalar.activation(out=C.s_mscr[:], in_=C.s_xmt[t][:], func=AF.Copy,
                             accum_out=C.s_mst[:, 0:1])
        ac.inc(nc.scalar.activation(out=C.s_mscr[:], in_=C.s_xmt[t][:], func=AF.Square,
                                    accum_out=C.s_mst[:, 1:2]))
        ac.wait(nc.vector)
        m = C.s_mst
        _vh(C, nc.vector.tensor_scalar(out=m[:, 2:3], in0=m[:, 0:1], scalar1=1.0 / D,
                                       scalar2=None, op0=ALU.mult))
        _vh(C, nc.vector.tensor_scalar(out=m[:, 3:4], in0=m[:, 1:2], scalar1=1.0 / D,
                                       scalar2=None, op0=ALU.mult))
        _vh(C, nc.vector.tensor_tensor(out=m[:, 4:5], in0=m[:, 2:3], in1=m[:, 2:3],
                                       op=ALU.mult))
        _vh(C, nc.vector.tensor_tensor(out=m[:, 5:6], in0=m[:, 3:4], in1=m[:, 4:5],
                                       op=ALU.subtract))
        _vh(C, nc.vector.tensor_scalar(out=m[:, 6:7], in0=m[:, 5:6], scalar1=1e-5,
                                       scalar2=None, op0=ALU.add))
        ve.inc(nc.vector.reciprocal(out=m[:, 7:8], in_=m[:, 6:7]))
        ve.wait(nc.scalar)
        ac.inc(nc.scalar.activation(out=m[:, 8:9], in_=m[:, 7:8], func=AF.Sqrt))
        ac.wait(nc.vector)
        if t == 1:
            sc.wait(nc.vector, xmT_sc)
        nc.vector.tensor_scalar(out=C.s_xmb[:], in0=C.s_xmt[t][:],
                                scalar1=m[:, 2:3], scalar2=m[:, 8:9],
                                op0=ALU.subtract, op1=ALU.mult)
        _age_ve(C)
        ve.wait(nc.scalar)
        xmT_sc = []
        for h in range(8):
            xmT_sc.append(sc.inc(nc.scalar.dma_start(
                out=C.s_xmT[:, h * 256 + t * 128: h * 256 + t * 128 + 128],
                in_=C.s_xmb[:, h * 128: h * 128 + 128], transpose=True)))

    sc.wait(nc.tensor, sc.snap())
    ac.wait(nc.tensor, C.a2_ac)
    for t in range(2):
        for h in range(8):
            mm = nc.tensor.matmul(
                pk(t, 9), C.s_xmT[:, h * 256 + t * 128: h * 256 + t * 128 + 128],
                C.s_gw[:, h * 9: h * 9 + 9], start=(h == 0), stop=(h == 7))
        te.inc(_age_te(C))
    te.wait(nc.vector)
    _age_ve(C)
    for t in range(2):
        ve.inc(nc.vector.tensor_tensor(out=C.s_g9[t][:], in0=pk(t, 9),
                                       in1=C.s_gbrep[:], op=ALU.add))
    g_ve = ve.n
    ve.wait(nc.scalar, g_ve)
    for t in range(2):
        ac.inc(nc.scalar.activation(out=C.s_g9[t][:], in_=C.s_g9[t][:], func=AF.Sigmoid))
    sig_ac = ac.n
    ac.wait(nc.vector, sig_ac)
    for t in range(2):
        _vh(C, nc.vector.tensor_tensor(out=C.s_e8[:], in0=C.s_g9[t][:, 1:9],
                                       in1=C.s_ebrep[:], op=ALU.add))
        _vh(C, nc.vector.max(out=C.s_m8[:], in_=C.s_e8[:]))
        _vh(C, nc.vector.tensor_scalar(out=C.s_oh[t][:], in0=C.s_e8[:],
                                       scalar1=C.s_m8[:, 0:1], scalar2=None,
                                       op0=ALU.is_equal))
        _vh(C, nc.vector.tensor_tensor(out=C.s_e8[:], in0=C.s_g9[t][:, 1:9],
                                       in1=C.s_oh[t][:], op=ALU.mult))
        _vh(C, nc.vector.reduce_sum(out=C.s_w2[:, 1:2], in_=C.s_e8[:],
                                    axis=mybir.AxisListType.X))
        ve.inc(nc.vector.tensor_copy(out=C.s_w2[:, 0:1], in_=C.s_g9[t][:, 0:1]))
        ve.wait(nc.scalar)
        ac.inc(nc.scalar.activation(out=C.s_w2e[:], in_=C.s_w2[:], func=AF.Exp))
        ac.wait(nc.vector)
        _vh(C, nc.vector.reduce_sum(out=C.s_sc1[:], in_=C.s_w2e[:],
                                    axis=mybir.AxisListType.X))
        _vh(C, nc.vector.reciprocal(out=C.s_sc1[:], in_=C.s_sc1[:]))
        _vh(C, nc.vector.tensor_tensor(out=C.s_wex[t][:], in0=C.s_w2e[:, 1:2],
                                       in1=C.s_sc1[:], op=ALU.mult))
        _vh(C, nc.vector.tensor_tensor(out=C.s_wshm[t][:], in0=C.s_w2e[:, 0:1],
                                       in1=C.s_sc1[:], op=ALU.mult))
        ve.inc(nc.vector.tensor_scalar(out=C.s_wshm[t][:], in0=C.s_wshm[t][:],
                                       scalar1=-1.0, scalar2=None, op0=ALU.add))
    gates_done = ve.n

    def ffn_T(lhs1, lhs2, bias1, bias2, w3, g1row, out_tiles, add_b1bc):
        for f in range(4):
            ve.wait(nc.tensor)
            ac.wait(nc.tensor)
            for h in range(8):
                nc.tensor.matmul(pk(0, 256),
                                 lhs1[:, h * HF + f * 128: h * HF + f * 128 + 128],
                                 C.s_xmT[:, h * 256: h * 256 + 256],
                                 start=(h == 0), stop=(h == 7))
                mm = nc.tensor.matmul(pk(1, 256),
                                      lhs2[:, h * HF + f * 128: h * HF + f * 128 + 128],
                                      C.s_xmT[:, h * 256: h * 256 + 256],
                                      start=(h == 0), stop=(h == 7))
            te.inc(_age_te(C))
            te.wait(nc.scalar)
            _age_sc(C)
            ac.inc(nc.scalar.activation(out=C.s_msil[:], in_=pk(0, 256), func=AF.Silu,
                                        bias=bias1[f]))
            ac.wait(nc.vector)
            nc.vector.scalar_tensor_tensor(out=C.s_mybf[:], in0=pk(1, 256),
                                           scalar=bias2[f], in1=C.s_msil[:],
                                           op0=ALU.add, op1=ALU.mult)
            ve.inc(nc.vector.tensor_scalar(out=C.s_mygs[:, f * 256: f * 256 + 256],
                                           in0=C.s_mybf[:],
                                           scalar1=C.s_cols[:, 80 + f: 81 + f],
                                           scalar2=None, op0=ALU.mult))
            ve.wait(nc.scalar)
            ac.inc(nc.scalar.activation(out=C.s_my2[:], in_=C.s_mybf[:], func=AF.Square))
            ac.wait(nc.tensor)
            nc.tensor.matmul(pk(4, 256, 1), C.s_onesc[:, 0:1], C.s_mybf[:],
                             start=(f == 0), stop=(f == 3))
            mm = nc.tensor.matmul(pk(5, 256, 1), C.s_onesc[:, 0:1], C.s_my2[:],
                                  start=(f == 0), stop=(f == 3))
            te.inc(mm)
            te.wait(nc.vector)
        _age_ve(C)
        nc.vector.tensor_scalar(out=C.s_mmu[:], in0=C.psum[0:1, 2048:2304],
                                scalar1=1.0 / HF, scalar2=None, op0=ALU.mult)
        nc.vector.tensor_scalar(out=C.s_mex2[:], in0=C.psum[0:1, 2560:2816],
                                scalar1=1.0 / HF, scalar2=None, op0=ALU.mult)
        nc.vector.tensor_tensor(out=C.s_mtmp[:], in0=C.s_mmu[:], in1=C.s_mmu[:],
                                op=ALU.mult)
        nc.vector.tensor_tensor(out=C.s_mtmp[:], in0=C.s_mex2[:], in1=C.s_mtmp[:],
                                op=ALU.subtract)
        nc.vector.tensor_scalar(out=C.s_mtmp[:], in0=C.s_mtmp[:], scalar1=1e-5,
                                scalar2=None, op0=ALU.add)
        ve.inc(nc.vector.reciprocal(out=C.s_mrin[:], in_=C.s_mtmp[:]))
        st_ev = ve.n
        ve.wait(nc.scalar, st_ev)
        nc.scalar.activation(out=C.s_mrst[:], in_=C.s_mrin[:], func=AF.Sqrt)
        nc.scalar.activation(out=C.s_mnmu[:], in_=C.s_mmu[:], func=AF.Copy, scale=-1.0)
        ac.inc(nc.scalar.activation(out=C.s_mrpk[0:1, :], in_=C.s_mrst[:], func=AF.Copy))
        rows_ac = ac.n
        ac.wait(nc.scalar, rows_ac)
        rc_sc = []
        for t in range(2):
            rc_sc.append(sc.inc(nc.scalar.dma_start(
                out=C.s_mrc[t][:], in_=C.s_mrpk[:, t * 128: t * 128 + 128],
                transpose=True)))
        ve.wait(nc.tensor, st_ev)
        a_done_ve = None
        for t in range(2):
            bA, bB = (2, 3)
            if a_done_ve is not None:
                ve.wait(nc.tensor, a_done_ve)
            for f in range(4):
                nc.tensor.matmul(pk(bA),
                                 C.s_mygs[:, f * 256 + t * 128: f * 256 + t * 128 + 128],
                                 w3[:, f * D: f * D + 512], start=(f == 0), stop=False)
                nc.tensor.matmul(pk(bB),
                                 C.s_mygs[:, f * 256 + t * 128: f * 256 + t * 128 + 128],
                                 w3[:, f * D + 512: f * D + 1024],
                                 start=(f == 0), stop=False)
            nc.tensor.matmul(pk(bA), C.s_mnmu[0:1, t * 128: t * 128 + 128],
                             g1row[0:1, 0:512], start=False, stop=True)
            mm = nc.tensor.matmul(pk(bB), C.s_mnmu[0:1, t * 128: t * 128 + 128],
                                  g1row[0:1, 512:1024], start=False, stop=True)
            te.inc(mm)
            a_te = te.n
            if t == 0:
                sc.wait(nc.vector, rc_sc)
            te.wait(nc.vector, a_te)
            _vh(C, nc.vector.tensor_copy(out=C.s_mrcf[t][:], in_=C.s_mrc[t][:, 0:1]))
            if add_b1bc:
                ve.inc(nc.vector.scalar_tensor_tensor(
                    out=out_tiles[t][:], in0=psum[:, 1024:2048],
                    scalar=C.s_mrcf[t][:, 0:1], in1=C.s_B1bc[:],
                    op0=ALU.mult, op1=ALU.add))
            else:
                ve.inc(nc.vector.tensor_scalar(
                    out=out_tiles[t][:], in0=psum[:, 1024:2048],
                    scalar1=C.s_mrcf[t][:, 0:1], scalar2=None, op0=ALU.mult))
            a_done_ve = ve.n
        out_ve = ve.n
        ve.wait(nc.tensor, out_ve)
        return out_ve

    sh1 = [C.s_cols[:, 0 + f:1 + f] for f in range(4)]
    sh2 = [C.s_cols[:, 4 + f:5 + f] for f in range(4)]

    def ffn_T_shared():
        for f in range(4):
            ve.wait(nc.tensor)
            ac.wait(nc.tensor)
            for h in range(8):
                nc.tensor.matmul(pk(0, 256),
                                 C.s_sw1[:, h * HF + f * 128: h * HF + f * 128 + 128],
                                 C.s_xmT[:, h * 256: h * 256 + 256],
                                 start=(h == 0), stop=(h == 7))
                mm = nc.tensor.matmul(pk(1, 256),
                                      C.s_sw2[:, h * HF + f * 128: h * HF + f * 128 + 128],
                                      C.s_xmT[:, h * 256: h * 256 + 256],
                                      start=(h == 0), stop=(h == 7))
            te.inc(_age_te(C))
            te.wait(nc.scalar)
            _age_sc(C)
            ac.inc(nc.scalar.activation(out=C.s_msil[:], in_=pk(0, 256), func=AF.Silu,
                                        bias=sh1[f]))
            ac.wait(nc.vector)
            nc.vector.scalar_tensor_tensor(out=C.s_mybf[:], in0=pk(1, 256),
                                           scalar=sh2[f], in1=C.s_msil[:],
                                           op0=ALU.add, op1=ALU.mult)
            ve.inc(nc.vector.tensor_scalar(out=C.s_mygs[:, f * 256: f * 256 + 256],
                                           in0=C.s_mybf[:],
                                           scalar1=C.s_cols[:, 8 + f: 9 + f],
                                           scalar2=None, op0=ALU.mult))
            ve.wait(nc.scalar)
            ac.inc(nc.scalar.activation(out=C.s_my2[:], in_=C.s_mybf[:], func=AF.Square))
            ac.wait(nc.tensor)
            nc.tensor.matmul(pk(4, 256, 1), C.s_onesc[:, 0:1], C.s_mybf[:],
                             start=(f == 0), stop=(f == 3))
            mm = nc.tensor.matmul(pk(5, 256, 1), C.s_onesc[:, 0:1], C.s_my2[:],
                                  start=(f == 0), stop=(f == 3))
            te.inc(mm)
            te.wait(nc.vector)
        _age_ve(C)
        nc.vector.tensor_scalar(out=C.s_mmu[:], in0=C.psum[0:1, 2048:2304],
                                scalar1=1.0 / HF, scalar2=None, op0=ALU.mult)
        nc.vector.tensor_scalar(out=C.s_mex2[:], in0=C.psum[0:1, 2560:2816],
                                scalar1=1.0 / HF, scalar2=None, op0=ALU.mult)
        nc.vector.tensor_tensor(out=C.s_mtmp[:], in0=C.s_mmu[:], in1=C.s_mmu[:],
                                op=ALU.mult)
        nc.vector.tensor_tensor(out=C.s_mtmp[:], in0=C.s_mex2[:], in1=C.s_mtmp[:],
                                op=ALU.subtract)
        nc.vector.tensor_scalar(out=C.s_mtmp[:], in0=C.s_mtmp[:], scalar1=1e-5,
                                scalar2=None, op0=ALU.add)
        ve.inc(nc.vector.reciprocal(out=C.s_mrin[:], in_=C.s_mtmp[:]))
        st_ev = ve.n
        ve.wait(nc.scalar, st_ev)
        nc.scalar.activation(out=C.s_mrst[:], in_=C.s_mrin[:], func=AF.Sqrt)
        nc.scalar.activation(out=C.s_mnmu[:], in_=C.s_mmu[:], func=AF.Copy, scale=-1.0)
        ac.inc(nc.scalar.activation(out=C.s_mrpk[0:1, :], in_=C.s_mrst[:], func=AF.Copy))
        rows_ac = ac.n
        ac.wait(nc.scalar, rows_ac)
        rc_sc = []
        for t in range(2):
            rc_sc.append(sc.inc(nc.scalar.dma_start(
                out=C.s_mrc[t][:], in_=C.s_mrpk[:, t * 128: t * 128 + 128],
                transpose=True)))
        ve.wait(nc.tensor, st_ev)
        a_done_ve = None
        for t in range(2):
            bA, bB = (2, 3)
            if a_done_ve is not None:
                ve.wait(nc.tensor, a_done_ve)
            for f in range(4):
                nc.tensor.matmul(pk(bA),
                                 C.s_mygs[:, f * 256 + t * 128: f * 256 + t * 128 + 128],
                                 C.s_sw3[:, f * D: f * D + 512],
                                 start=(f == 0), stop=False)
                nc.tensor.matmul(pk(bB),
                                 C.s_mygs[:, f * 256 + t * 128: f * 256 + t * 128 + 128],
                                 C.s_sw3[:, f * D + 512: f * D + 1024],
                                 start=(f == 0), stop=False)
            nc.tensor.matmul(pk(bA), C.s_mnmu[0:1, t * 128: t * 128 + 128],
                             C.s_G1b[0:1, 0:512], start=False, stop=True)
            mm = nc.tensor.matmul(pk(bB), C.s_mnmu[0:1, t * 128: t * 128 + 128],
                                  C.s_G1b[0:1, 512:1024], start=False, stop=True)
            te.inc(mm)
            a_te = te.n
            if t == 0:
                sc.wait(nc.vector, rc_sc)
            te.wait(nc.vector, a_te)
            _vh(C, nc.vector.tensor_copy(out=C.s_mrcf[t][:], in_=C.s_mrc[t][:, 0:1]))
            ve.inc(nc.vector.scalar_tensor_tensor(
                out=C.s_sm[t][:], in0=psum[:, 1024:2048],
                scalar=C.s_mrcf[t][:, 0:1], in1=C.s_B1bc[:],
                op0=ALU.mult, op1=ALU.add))
            a_done_ve = ve.n
        out_ve = ve.n
        ve.wait(nc.tensor, out_ve)
        return out_ve

    ffn_T_shared()

    ve.wait(nc.vector, gates_done)
    for t in range(2):
        nc.vector.memset(C.s_acc[t][:], 0.0)
    ve.inc(nc.vector.memset(C.s_pk128[:], 0.0))
    gp.wait(nc.scalar, C.ew_ld0)
    ew_relay = _age_sc(C)
    ac.wait(nc.tensor, ew_relay)
    gp.wait(nc.tensor, C.ew_ld0)
    ew_free = {}
    for e in range(NE):
        sl = e % 2
        if e >= 2:
            te.wait(nc.gpsimd, ew_free[e - 2])
            ewt = [C.wload(C.s_ew1[sl], C.ti["ew1T"], HF, 8, e),
                   C.wload(C.s_ew2[sl], C.ti["ew2T"], HF, 8, e),
                   C.wload(C.s_ew3[sl], C.ti["ew3T"], D, 4, e)]
            gp.wait(nc.scalar, ewt)
            ew_relay = _age_sc(C)
            ac.wait(nc.tensor, ew_relay)
            gp.wait(nc.tensor, ewt)
        nc.vector.memset(C.s_selbE[:], 0.0)
        for f in range(4):
            inst = nc.vector.tensor_copy(out=C.s_selbE[:, f * 8 + e: f * 8 + e + 1],
                                         in_=C.s_colsb[:, 84 + f:85 + f])
        ve.inc(inst)
        selbe_ve = ve.n
        ve.wait(nc.tensor, selbe_ve)
        for f in range(4):
            for half in range(2):
                nc.tensor.matmul(pk(6 + half, 512, 8), C.s_selbE[:, f * 8: f * 8 + 8],
                                 C.s_ew3[sl][:, f * D + half * 512: f * D + half * 512 + 512],
                                 start=(e == 0 and f == 0), stop=(e == 7 and f == 3),
                                 skip_group_check=True)
        for f in range(4):
            for half in range(2):
                mm = nc.tensor.matmul(pk(2 + half, 512, 1), C.s_colsb[:, 80 + f:81 + f],
                                      C.s_ew3[sl][:, f * D + half * 512: f * D + half * 512 + 512],
                                      start=(f == 0), stop=(f == 3))
        te.inc(mm)
        te.wait(nc.vector)
        _age_ve(C)
        nc.vector.tensor_copy(out=C.s_g1e[0:1, 0:512], in_=psum[0:1, 1024:1536])
        ve.inc(nc.vector.tensor_copy(out=C.s_g1e[0:1, 512:1024],
                                     in_=psum[0:1, 1536:2048]))
        ve.wait(nc.tensor)
        eb1c = [C.s_cols[:, 16 + 4 * e + f: 17 + 4 * e + f] for f in range(4)]
        eb2c = [C.s_cols[:, 48 + 4 * e + f: 49 + 4 * e + f] for f in range(4)]
        ffn_T(C.s_ew1[sl], C.s_ew2[sl], eb1c, eb2c, C.s_ew3[sl], C.s_g1e,
              C.s_moeA, False)
        ew_free[e] = te.n
        for t in range(2):
            _vh(C, nc.vector.tensor_tensor(out=C.s_ohew[t][:, e:e + 1],
                                           in0=C.s_oh[t][:, e:e + 1],
                                           in1=C.s_wex[t][:], op=ALU.mult))
            ve.inc(nc.vector.scalar_tensor_tensor(
                out=C.s_acc[t][:], in0=C.s_moeA[t][:],
                scalar=C.s_ohew[t][:, e:e + 1],
                in1=C.s_acc[t][:], op0=ALU.mult, op1=ALU.add))
    comb_done = ve.n

    te.wait(nc.vector)
    _age_ve(C)
    ve.inc(nc.vector.tensor_tensor(out=C.s_b1e[:], in0=psum[0:8, 3072:4096],
                                   in1=C.s_eb3r[:], op=ALU.add))
    b1e_ve = ve.n

    for t in range(2):
        nc.vector.tensor_copy(out=C.s_pk128[:, 0:8], in_=C.s_ohew[t][:])
        _age_ve(C)
        ve.wait(nc.scalar)
        pkt = sc.inc(nc.scalar.dma_start(out=C.s_pkT[t][:], in_=C.s_pk128[:],
                                         transpose=True))
        sc.wait(nc.tensor, pkt)
        ve.wait(nc.tensor)
        nc.tensor.matmul(pk(0), C.s_pkT[t][0:8, 0:128], C.s_b1e[0:8, 0:512],
                         start=True, stop=True)
        mm = nc.tensor.matmul(pk(1), C.s_pkT[t][0:8, 0:128], C.s_b1e[0:8, 512:1024],
                              start=True, stop=True)
        te.inc(mm)
        te.wait(nc.vector)
        nc.vector.scalar_tensor_tensor(out=C.s_sm[t][:], in0=C.s_sm[t][:],
                                       scalar=C.s_wshm[t][:, 0:1], in1=C.s_acc[t][:],
                                       op0=ALU.mult, op1=ALU.add)
        ve.inc(nc.vector.tensor_tensor(out=C.s_sm[t][:], in0=C.s_sm[t][:],
                                       in1=psum[:, 0:1024], op=ALU.add))
        ve.wait(nc.tensor)
    _age_ve(C)
    delta_ve = ve.n
    ve.wait(nc.gpsimd, delta_ve)
    C.patch_gp = []
    for k, (b0, r) in enumerate(TILE_ORDER):
        srcp = C.s_sm[k // 2][64 * (k % 2): 64 * (k % 2) + 64, :]
        dst = bass.AP(C.i_xp, ((r * B + b0) * 128) * D, [[2 * D, 64], [1, D]])
        C.patch_gp.append(gp.inc(nc.gpsimd.dma_start(out=dst, in_=srcp,
                                                     accum_op=ALU.add)))
    C.moe_te = C.sem["te"].n
    C.moe_ve = C.sem["ve"].n
    C.moe_sc = C.sem["sc"].snap()


def _emit_2B(C, c):
    nc = C.nc
    ld, sc, gp, te, ve, ac, st = (C.sem[n] for n in
                                  ["ld", "sc", "gp", "te", "ve", "ac", "st"])
    pk = C.pk
    psum = C.psum
    SR, umax, nb_of, cum = _cum_of(c)
    amax = umax // 8
    coll = C.sem["coll"]

    te.wait(nc.sync, C.moe_te)
    ve.wait(nc.sync, C.moe_ve)
    sc.wait(nc.sync, C.moe_sc)
    te.wait(nc.vector, C.moe_te)
    te.wait(nc.gpsimd, C.moe_te)

    if C.dbg:
        ve.wait(nc.gpsimd, C.a2_ve)
        C.sem["st"].inc(nc.gpsimd.dma_start(out=C.d_den.ap(), in_=C.s_denom[:]))
        C.sem["st"].wait(nc.vector, C.sem["st"].snap()[-0 - 1])
        for j, v in C.sem["st"].snap():
            if v:
                C.sem["st"].wait(nc.vector, (j, v))
    ve.inc(nc.vector.reciprocal(out=C.s_denom[:, 0:cum[umax + 1]],
                                in_=C.s_denom[:, 0:cum[umax + 1]]))
    denr = ve.n

    coll.wait(nc.gpsimd, C.coll_s)
    ve.wait(nc.tensor, denr)
    for b in range(B):
        if b >= 1:
            te.wait(nc.gpsimd)
        sbtok = gp.inc(nc.gpsimd.dma_start(
            out=C.s_sb2[:], in_=bass.AP(C.i_sg, b * D, [[8 * D, 32], [1, D]])))
        gp.wait(nc.vector, sbtok)
        sb_aged = _age_ve(C)
        ve.wait(nc.tensor, sb_aged)
        gp.wait(nc.tensor, sbtok)
        nc.tensor.matmul(pk(0, 512, 32), C.s_uselp[:, b * 32: b * 32 + 32],
                         C.s_sb2[:, 0:512], start=(b == 0), stop=(b == 7),
                         skip_group_check=True)
        mm = nc.tensor.matmul(pk(1, 512, 32), C.s_uselp[:, b * 32: b * 32 + 32],
                              C.s_sb2[:, 512:1024], start=(b == 0), stop=(b == 7),
                              skip_group_check=True)
        te.inc(_age_te(C) if b == 7 else mm)
    te.wait(nc.scalar)
    ac.inc(nc.scalar.activation(out=C.s_sufb[:], in_=psum[0:32, 0:1024],
                                func=AF.Copy, scale=1e-9))
    sufb_ac = ac.n
    coll.wait(nc.sync, C.coll_v)

    erd_ld, erd_ve, vr_ld, vr_te, att_te = {}, {}, {}, {}, {}
    x2p_ve, ost_st = {}, {}
    fin_ve = ve.n
    av_prev = 0
    for b in range(B):
        if b == 0:
            ac.wait(nc.tensor, sufb_ac)
        else:
            ve.wait(nc.tensor, fin_ve)
            te.wait(nc.vector, av_prev)
        for u in range(umax + 1):
            nb = nb_of(u)
            a = u // 8
            g = b * (amax + 1) + a
            off_g = cum[u] - cum[8 * a]
            if u % 8 == 0:
                if g >= 2:
                    ve.wait(nc.sync, erd_ve[g - 2])
                W = cum[min(8 * a + 7, umax) + 1] - cum[8 * a]
                erd_ld[g] = ld.inc(nc.sync.dma_start(
                    out=C.s_erd[g % 2][:, 0:W],
                    in_=C.i_E.ap()[b][:, cum[8 * a]: cum[8 * a] + W]))
            vdx = len(vr_ld)
            vsl = vdx % 4
            if vdx >= 4:
                te.wait(nc.sync, vr_te[vdx - 4])
            sblk = _blk(u)
            vtok = ld.inc(nc.sync.dma_start(
                out=C.s_vr[vsl][:],
                in_=bass.AP(C.i_gb, C.vg_off + (sblk * 8 + b) * 128 * D, [[D, 128], [1, D]])))
            vr_ld[(b, u)] = (vtok, vdx, vsl)
            if u % 8 == 0:
                ld.wait(nc.vector, erd_ld[g])
                _age_ve(C)
                if u == 0:
                    ve.wait(nc.vector, denr)
            if u >= 2:
                te.wait(nc.vector, att_te[(b, u - 2)])
            vln, vdx2, vsl2 = vr_ld[(b, u)]
            ld.wait(nc.vector, vln)
            _age_ve(C)
            ve.inc(nc.vector.tensor_tensor(
                out=C.s_att[u % 2][:, 0:nb * 128],
                in0=C.s_erd[g % 2][:, off_g: off_g + nb * 128],
                in1=C.s_denom[:, cum[u]: cum[u] + nb * 128], op=ALU.mult))
            if u == SR[nb - 1]:
                nc.vector.tensor_tensor(
                    out=C.s_att[u % 2][:, (nb - 1) * 128: nb * 128],
                    in0=C.s_att[u % 2][:, (nb - 1) * 128: nb * 128],
                    in1=C.s_trilM[:], op=ALU.mult)
                ve.inc(nc.vector.tensor_tensor(
                    out=C.s_att[u % 2][:, (nb - 1) * 128: nb * 128],
                    in0=C.s_att[u % 2][:, (nb - 1) * 128: nb * 128],
                    in1=C.s_trilC[:], op=ALU.add))
            att_rdy = ve.n
            ve.wait(nc.tensor, att_rdy)
            ld.wait(nc.tensor, vln)
            for r in range(nb):
                for half in range(2):
                    mm = nc.tensor.matmul(
                        psum[:, r * 1024 + half * 512: r * 1024 + half * 512 + 512],
                        C.s_att[u % 2][:, r * 128: r * 128 + 128],
                        C.s_vr[vsl2][:, half * 512: half * 512 + 512],
                        start=(u == 0), stop=(u == SR[r]))
            te.inc(mm)
            att_te[(b, u)] = te.n
            vr_te[vdx2] = te.n
            if u == min(8 * a + 7, umax):
                erd_ve[g] = ve.n
        for r in range(4):
            j = b * 4 + r
            for half in range(2):
                mm = nc.tensor.matmul(
                    psum[:, r * 1024 + half * 512: r * 1024 + half * 512 + 512],
                    C.s_sel32[:, j * 128: j * 128 + 128],
                    C.s_sufb[0:32, half * 512: half * 512 + 512],
                    start=False, stop=True)
        te.inc(mm)
        av_te = te.n
        av_prev = av_te
        for r in range(4):
            i = b * 4 + r
            if i >= 2:
                ve.wait(nc.sync, x2p_ve[i - 2])
            if i == 0:
                gp.wait(nc.sync, C.patch_gp)
            xld = ld.inc(nc.sync.dma_start(out=C.s_x2p[r % 2][:],
                                           in_=C.i_xp.ap()[r, b]))
            te.wait(nc.vector, av_te)
            ld.wait(nc.vector, xld)
            _age_ve(C)
            if i >= 2:
                st.wait(nc.vector, ost_st[i - 2])
            nc.vector.tensor_tensor(out=C.s_ost[r % 2][:],
                                    in0=psum[:, r * 1024: r * 1024 + 1024],
                                    in1=C.s_x2p[r % 2][:], op=ALU.add)
            ve.inc(nc.vector.memset(C.s_agev[:], 0.0))
            x2p_ve[i] = ve.n
            ve.wait(nc.gpsimd)
            ost_st[i] = st.inc(nc.gpsimd.dma_start(out=C.t_out.ap()[r, b],
                                                   in_=C.s_ost[r % 2][:]))
        fin_ve = ve.n
    if C.dbg:
        st.inc(nc.gpsimd.dma_start(out=C.d_ks.ap(), in_=C.i_kshard.ap()))
        st.inc(nc.gpsimd.dma_start(out=C.d_vs.ap(), in_=C.i_vshard.ap()))
        st.inc(nc.gpsimd.dma_start(out=C.d_ss.ap(), in_=C.i_sshard.ap()))
    st.wait_all(nc.gpsimd)


def kernel(**inputs):
    in_maps = _host_prepare(inputs)
    nc = _build()
    res = run_bass_kernel_spmd(nc, in_maps, core_ids=list(range(NCORE)))
    out = np.empty((B, 32 * 128, D), np.float32)
    for c in range(NCORE):
        SR = _strips_desc(c)
        arr = np.asarray(res.results[c]["out"])
        for r in range(4):
            out[:, SR[r] * 128:(SR[r] + 1) * 128, :] = arr[r]
    return out


# version 05
# speedup vs baseline: 1.2063x; 1.2063x over previous
import numpy as np
import ml_dtypes
from contextlib import ExitStack

import concourse.bass as bass
import concourse.mybir as mybir
from concourse.bass_utils import run_bass_kernel_spmd

F32 = mybir.dt.float32
BF16 = mybir.dt.bfloat16
AF = mybir.ActivationFunctionType
ALU = mybir.AluOpType

NCORE = 8
USE_SHARED = True
B = 8
D = 1024
HF = 512
NE = 8
BF = ml_dtypes.bfloat16
TILE_ORDER = [(0, 3), (0, 2), (1, 3), (1, 2)]


def _strips_desc(c):
    return [31 - 2 * c, 30 - 2 * c, 2 * c + 1, 2 * c]


def _strip_owner(s):
    if s < 16:
        return s // 2, s % 2
    return (31 - s) // 2, 2 + (s % 2)


def _blk(s):
    c, l = _strip_owner(s)
    return c * 4 + l


class SemT:
    def __init__(self, nc, name):
        self.h = nc.alloc_semaphore(name)
        self.n = 0

    def inc(self, inst, k=1):
        inst.then_inc(self.h, k)
        self.n += k
        return self.n

    def wait(self, eng, n=None):
        eng.wait_ge(self.h, self.n if n is None else n)


class QSem:

    LANES = 8

    def __init__(self, nc, name):
        self.sems = [nc.alloc_semaphore(f"{name}{j}") for j in range(self.LANES)]
        self.cnt = [0] * self.LANES
        self.idx = 0

    def inc(self, inst, k=16):
        j = self.idx % self.LANES
        self.idx += 1
        inst.then_inc(self.sems[j], k)
        self.cnt[j] += k
        return (j, self.cnt[j])

    def wait(self, eng, tok):
        if isinstance(tok, list):
            for t in tok:
                self.wait(eng, t)
            return
        j, v = tok
        eng.wait_ge(self.sems[j], v)

    def snap(self):
        return [(j, self.cnt[j]) for j in range(self.LANES)]

    def wait_all(self, eng):
        for j, v in self.snap():
            if v:
                eng.wait_ge(self.sems[j], v)

    def state(self):
        return (self.idx, list(self.cnt))

    def restore(self, st):
        self.idx, cnt = st
        self.cnt = list(cnt)



def _host_prepare(inp):
    x = np.asarray(inp["L0_attn_x"], np.float32)
    ib = np.asarray(inp["L0_ffn_index_b"]).astype(np.int64)
    it = np.asarray(inp["L0_ffn_index_t"]).astype(np.int64)
    M = ib.shape[0]

    def T(a):
        return np.ascontiguousarray(np.asarray(a, np.float32).T)

    shared = dict(
        wqT=T(inp["L0_attn_Wq"]), wkT=T(inp["L0_attn_Wk"]), wvT=T(inp["L0_attn_Wv"]),
        sw1T=T(inp["L0_ffn_sw1"]), sw2T=T(inp["L0_ffn_sw2"]), sw3T=T(inp["L0_ffn_sw3"]),
        gwT=np.ascontiguousarray(np.concatenate(
            [np.asarray(inp["L0_ffn_sg_w"], np.float32),
             np.asarray(inp["L0_ffn_eg_w"], np.float32)], axis=0).T),
        ew1T=np.ascontiguousarray(np.asarray(inp["L0_ffn_ew1"], np.float32).transpose(0, 2, 1)),
        ew2T=np.ascontiguousarray(np.asarray(inp["L0_ffn_ew2"], np.float32).transpose(0, 2, 1)),
        ew3T=np.ascontiguousarray(np.asarray(inp["L0_ffn_ew3"], np.float32).transpose(0, 2, 1)),
    )

    def col4(v, n):
        return np.ascontiguousarray(np.asarray(v, np.float32).reshape(n, 128).T)

    cols = np.zeros((128, 88), np.float32)
    cols[:, 0:4] = col4(inp["L0_ffn_sb1"], 4)
    cols[:, 4:8] = col4(inp["L0_ffn_sb2"], 4)
    cols[:, 8:12] = col4(inp["L0_ffn_s_ln_g"], 4)
    cols[:, 12:16] = col4(inp["L0_ffn_s_ln_b"], 4)
    eb1 = np.asarray(inp["L0_ffn_eb1"], np.float32)
    eb2 = np.asarray(inp["L0_ffn_eb2"], np.float32)
    for e in range(NE):
        cols[:, 16 + 4 * e:20 + 4 * e] = col4(eb1[e], 4)
        cols[:, 48 + 4 * e:52 + 4 * e] = col4(eb2[e], 4)
    cols[:, 80:84] = col4(inp["L0_ffn_sel_ln_g"], 4)
    cols[:, 84:88] = col4(inp["L0_ffn_sel_ln_b"], 4)
    shared["cols"] = cols

    rows16 = np.zeros((16, D), np.float32)
    rows16[0] = np.asarray(inp["L0_ffn_sb3"], np.float32)
    rows16[1:9] = np.asarray(inp["L0_ffn_eb3"], np.float32)
    gb9 = np.concatenate([np.asarray(inp["L0_ffn_sg_b"], np.float32).reshape(-1),
                          np.asarray(inp["L0_ffn_eg_b"], np.float32).reshape(-1)])
    rows16[9, 0:9] = gb9
    shared["rows16"] = rows16
    shared["gb_rep"] = np.ascontiguousarray(
        np.broadcast_to(gb9[None, :], (128, 9)).astype(np.float32))
    shared["expb_rep"] = np.ascontiguousarray(np.broadcast_to(
        np.asarray(inp["L0_ffn_exp_bias"], np.float32).reshape(1, 8), (128, 8)))

    pp = np.arange(128)[:, None]
    cc = np.arange(128)[None, :]
    shared["trilM"] = (pp <= cc).astype(BF)
    shared["trilC"] = (np.float32(1e-9) * (pp > cc)).astype(BF)
    sel32 = np.zeros((32, 32 * 128), np.float32)
    for j in range(32):
        sel32[j, j * 128:(j + 1) * 128] = 1.0
    shared["sel32"] = sel32.astype(BF)
    sel128 = np.zeros((128, 32 * 32), np.float32)
    for i in range(32):
        sel128[:, i * 32 + (3 - i % 4) * 8 + i // 4] = 1.0
    shared["sel128"] = sel128.astype(BF)
    shared["eb3r"] = np.asarray(inp["L0_ffn_eb3"], np.float32)

    slots = [dict() for _ in range(NCORE)]
    for m in range(M):
        b0, t0 = int(ib[m]), int(it[m])
        stp = t0 // 128
        if stp >= 16 or b0 not in (0, 1) or (t0 % 2) != 0:
            raise ValueError("unsupported moe index pattern")
        c, l = _strip_owner(stp)
        key = (b0, 3 - l)
        slots[c].setdefault(key, {})
        if (t0 % 128) in slots[c][key]:
            raise ValueError("duplicate moe index")
        slots[c][key][t0 % 128] = m
    for c in range(NCORE):
        if sorted(slots[c].keys()) != sorted(TILE_ORDER):
            raise ValueError("unsupported moe index pattern (tiles)")
        for key in TILE_ORDER:
            if sorted(slots[c][key].keys()) != list(range(0, 128, 2)):
                raise ValueError("unsupported moe index pattern (rows)")

    in_maps = []
    for c in range(NCORE):
        SR = _strips_desc(c)
        xs = np.empty((4, B, 128, D), np.float32)
        for r in range(4):
            xs[r] = x[:, SR[r] * 128:(SR[r] + 1) * 128, :]
        xmoe = np.empty((256, D), np.float32)
        for k, (b0, r) in enumerate(TILE_ORDER):
            stp = SR[r]
            for j in range(64):
                xmoe[k * 64 + j] = x[b0, stp * 128 + 2 * j]
        uselp = np.zeros((32, 256), np.float32)
        for sp in range(32):
            co, lo = _strip_owner(sp)
            for b0 in range(B):
                for r in range(4):
                    if sp > SR[r]:
                        uselp[co * 4 + lo, b0 * 32 + (b0 * 4 + r)] = 1.0
        m = dict(shared)
        m.update(xs=xs, xmoe=xmoe, uselp=uselp)
        in_maps.append(m)
    return in_maps



class Ctx:
    pass


def _build(dbg=False):
    nc = bass.Bass("TRN2", target_bir_lowering=False)
    C = Ctx()
    C.nc = nc
    ik = "ExternalOutput" if dbg else "Internal"

    ti = {}
    for name, shape, dt in [
        ("xs", [4, B, 128, D], F32), ("xmoe", [256, D], F32), ("uselp", [32, 256], F32),
        ("wqT", [D, D], F32), ("wkT", [D, D], F32), ("wvT", [D, D], F32),
        ("sw1T", [D, HF], F32), ("sw2T", [D, HF], F32), ("sw3T", [HF, D], F32),
        ("gwT", [D, 9], F32),
        ("ew1T", [NE, D, HF], F32), ("ew2T", [NE, D, HF], F32), ("ew3T", [NE, HF, D], F32),
        ("cols", [128, 88], F32), ("rows16", [16, D], F32),
        ("gb_rep", [128, 9], F32), ("expb_rep", [128, 8], F32),
        ("trilM", [128, 128], BF16), ("trilC", [128, 128], BF16),
        ("sel32", [32, 32 * 128], BF16), ("sel128", [128, 32 * 32], BF16),
        ("eb3r", [8, D], F32),
    ]:
        ti[name] = nc.dram_tensor(name, shape, dt, kind="ExternalInput")
    C.ti = ti
    C.t_out = nc.dram_tensor("out", [4, B, 128, D], F32, kind="ExternalOutput")

    C.dbg = dbg
    C.i_kshard = nc.dram_tensor("kshard", [4 * B * 128, D], BF16, kind="Internal")
    C.i_vshard = nc.dram_tensor("vshard", [4 * B * 128, D], BF16, kind="Internal")
    C.i_sshard = nc.dram_tensor("sshard", [32, D], F32, kind="Internal")
    NKV = NCORE * 4 * B * 128
    C.i_gb = nc.dram_tensor("gblob", [2 * NKV, D], BF16, kind="Internal",
                            **(dict(addr_space="Shared") if USE_SHARED else {}))
    C.kg_off = 0
    C.vg_off = NKV * D
    C.i_sg = nc.dram_tensor("sgath", [NCORE * 32, D], F32, kind="Internal")
    C.i_fl = nc.dram_tensor("flagin", [1, 16], F32, kind="Internal")
    C.i_flo = nc.dram_tensor("flagout", [1, 16], F32, kind="Internal")
    C.i_xp = nc.dram_tensor("xpart", [4, B, 128, D], F32, kind=ik)
    C.i_qT = nc.dram_tensor("qTh", [B, 128, 8 * 4 * 128], BF16, kind=ik)
    C.i_E = nc.dram_tensor("Ebuf", [B, 128, 8448], BF16, kind=ik)
    if dbg:
        C.d_ks = nc.dram_tensor("d_kshard", [4 * B * 128, D], BF16, kind="ExternalOutput")
        C.d_vs = nc.dram_tensor("d_vshard", [4 * B * 128, D], BF16, kind="ExternalOutput")
        C.d_ss = nc.dram_tensor("d_sshard", [32, D], F32, kind="ExternalOutput")
        C.d_kg = nc.dram_tensor("d_kgath", [NCORE * 1024, D], BF16, kind="ExternalOutput")
        C.d_den = nc.dram_tensor("d_denom", [128, 8448], F32, kind="ExternalOutput")

    stk = ExitStack()
    C.stk = stk

    def sb(name, shape, dtype):
        return stk.enter_context(nc.sbuf_tensor(name, shape, dtype))

    C.s_cols = sb("s_cols", [128, 88], F32)
    C.s_agev = sb("s_agev", [128, 64], BF16)
    C.s_ages = sb("s_ages", [128, 16], F32)
    C.s_colsb = sb("s_colsb", [128, 88], BF16)
    C.s_rows = sb("s_rows", [16, D], F32)
    C.s_gbrep = sb("s_gbrep", [128, 9], F32)
    C.s_ebrep = sb("s_ebrep", [128, 8], F32)
    C.s_trilM = sb("s_trilM", [128, 128], BF16)
    C.s_trilC = sb("s_trilC", [128, 128], BF16)
    C.s_ones1 = sb("s_ones1", [1, 128], BF16)
    C.s_onesc = sb("s_onesc", [128, 1], BF16)
    C.s_BG = sb("s_BG", [2, D], F32)
    C.s_B1b = sb("s_B1b", [1, D], BF16)
    C.s_G1b = sb("s_G1b", [1, D], BF16)
    C.s_sel32 = sb("s_sel32", [32, 32 * 128], BF16)
    C.s_B1bc = sb("s_B1bc", [128, D], F32)
    C.s_sufb = sb("s_sufb", [32, D], BF16)
    C.s_uselp = sb("s_uselp", [32, 256], BF16)
    C.s_sel128 = sb("s_sel128", [128, 32 * 32], BF16)
    C.s_eb3r = sb("s_eb3r", [8, D], F32)
    C.s_xsumr = sb("s_xsumr", [32, D], BF16)
    C.s_xsumT = sb("s_xsumT", [128, 8 * 32], BF16)

    C.psum = stk.enter_context(nc.psum_tensor("ps", [128, 4096], F32))

    def pk(k, n=512, p=128):
        return C.psum[0:p, 512 * k: 512 * k + n]

    C.pk = pk

    C.sem = {}
    for n in ["ld", "sc", "gp", "st"]:
        C.sem[n] = QSem(nc, n)
    for n in ["te", "ve", "ac", "coll"]:
        C.sem[n] = SemT(nc, n)
    for s in C.sem.values():
        if isinstance(s, QSem):
            for h in s.sems:
                nc.gpsimd.sem_clear(h)
        else:
            nc.gpsimd.sem_clear(s.h)
    nc.all_engine_barrier()

    _emit_prologue(C)
    _emit_phase1(C)
    _emit_collectives(C)

    snap = {k: (s.state() if isinstance(s, QSem) else s.n)
            for k, s in C.sem.items()}
    engines = [nc.tensor, nc.scalar, nc.vector, nc.gpsimd, nc.sync]
    idx = [eng.compute_val(nc.partition_id([eng.engine])) for eng in engines]
    for c in nc.Switch(engines=engines, index=idx, n=NCORE):
        for k, s in C.sem.items():
            if isinstance(s, QSem):
                s.restore(snap[k])
            else:
                s.n = snap[k]
        _emit_2A(C, c)
        _emit_moe(C)
        _emit_2B(C, c)
    stk.close()
    return nc


def _emit_prologue(C):
    nc = C.nc
    ld, sc, gp, te, ve, ac = (C.sem[n] for n in ["ld", "sc", "gp", "te", "ve", "ac"])
    pk = C.pk
    ti = C.ti
    psum = C.psum

    for dst, src in [(C.s_cols, "cols"), (C.s_rows, "rows16"), (C.s_gbrep, "gb_rep"),
                     (C.s_ebrep, "expb_rep"), (C.s_trilM, "trilM"), (C.s_trilC, "trilC"),
                     (C.s_sel32, "sel32"), (C.s_sel128, "sel128"),
                     (C.s_eb3r, "eb3r")]:
        ld.inc(nc.sync.dma_start(out=dst[:], in_=ti[src].ap()))
    C.ld_consts = ld.snap()

    sb = lambda name, shape, dtype: C.stk.enter_context(nc.sbuf_tensor(name, shape, dtype))
    C.s_sw1 = sb("s_sw1", [128, 8 * HF], BF16)
    C.s_sw2 = sb("s_sw2", [128, 8 * HF], BF16)
    C.s_sw3 = sb("s_sw3", [128, 4 * D], BF16)
    C.s_gw = sb("s_gw", [128, 8 * 9], BF16)
    C.wscope = ExitStack()
    C.s_wq = C.wscope.enter_context(nc.sbuf_tensor("s_wq", [128, 8 * D], BF16))
    C.s_wk = C.wscope.enter_context(nc.sbuf_tensor("s_wk", [128, 8 * D], BF16))
    C.s_wv = C.wscope.enter_context(nc.sbuf_tensor("s_wv", [128, 8 * D], BF16))

    def wload(dst, src_t, ncol, nchunk, e=None):
        off = 0 if e is None else e * 128 * nchunk * ncol
        src = bass.AP(src_t, off, [[ncol, 128], [128 * ncol, nchunk], [1, ncol]])
        return gp.inc(nc.gpsimd.dma_start(out=dst[:], in_=src))

    C.wload = wload
    wload(C.s_wq, ti["wqT"], D, 8)
    wload(C.s_wk, ti["wkT"], D, 8)
    wload(C.s_wv, ti["wvT"], D, 8)
    wload(C.s_sw1, ti["sw1T"], HF, 8)
    wload(C.s_sw2, ti["sw2T"], HF, 8)
    wload(C.s_sw3, ti["sw3T"], D, 4)
    wload(C.s_gw, ti["gwT"], 9, 8)
    gp.inc(nc.gpsimd.dma_start(out=C.s_uselp[:], in_=ti["uselp"].ap()))
    C.gp_weights = gp.snap()

    ld.wait(nc.vector, C.ld_consts)
    nc.vector.memset(C.s_ones1[:], 1.0)
    nc.vector.memset(C.s_onesc[:], 1.0)
    ve.inc(nc.vector.tensor_copy(out=C.s_colsb[:], in_=C.s_cols[:]))
    C.ve_consts = ve.n

    gp.wait(nc.scalar, C.gp_weights)
    ld.wait(nc.scalar, C.ld_consts)
    relay = _age_sc(C)
    ac.wait(nc.tensor, relay)
    gp.wait(nc.tensor, C.gp_weights)
    ve.wait(nc.tensor, C.ve_consts)
    for f in range(4):
        for h in range(2):
            nc.tensor.matmul(pk(0 + h, 512, 1), C.s_colsb[:, 12 + f:13 + f],
                             C.s_sw3[:, f * D + h * 512: f * D + h * 512 + 512],
                             start=(f == 0), stop=(f == 3))
            mm = nc.tensor.matmul(pk(2 + h, 512, 1), C.s_colsb[:, 8 + f:9 + f],
                                  C.s_sw3[:, f * D + h * 512: f * D + h * 512 + 512],
                                  start=(f == 0), stop=(f == 3))
    te.inc(mm)
    te.wait(nc.vector)
    nc.vector.tensor_tensor(out=C.s_BG[0:1, 0:512], in0=psum[0:1, 0:512],
                            in1=C.s_rows[0:1, 0:512], op=ALU.add)
    nc.vector.tensor_tensor(out=C.s_BG[0:1, 512:1024], in0=psum[0:1, 512:1024],
                            in1=C.s_rows[0:1, 512:1024], op=ALU.add)
    nc.vector.tensor_copy(out=C.s_G1b[0:1, 0:512], in_=psum[0:1, 1024:1536])
    nc.vector.tensor_copy(out=C.s_G1b[0:1, 512:1024], in_=psum[0:1, 1536:2048])
    ve.inc(nc.vector.tensor_copy(out=C.s_B1b[:], in_=C.s_BG[0:1, :]))
    ve.wait(nc.tensor)
    nc.tensor.matmul(pk(4), C.s_ones1[0:1, :], C.s_B1b[0:1, 0:512], start=True, stop=True)
    te.inc(nc.tensor.matmul(pk(5), C.s_ones1[0:1, :], C.s_B1b[0:1, 512:1024],
                            start=True, stop=True))
    te.wait(nc.vector)
    nc.vector.tensor_copy(out=C.s_B1bc[:, 0:512], in_=pk(4))
    ve.inc(nc.vector.tensor_copy(out=C.s_B1bc[:, 512:1024], in_=pk(5)))
    C.ve_b1bc = ve.n


def _emit_phase1(C):
    nc = C.nc
    ld, sc, gp, te, ve, ac = (C.sem[n] for n in ["ld", "sc", "gp", "te", "ve", "ac"])
    pk = C.pk
    psum = C.psum

    p1 = ExitStack()

    def sp(name, shape, dtype):
        return p1.enter_context(nc.sbuf_tensor(name, shape, dtype))

    s_xt = [sp(f"s_xt{i}", [128, D], F32) for i in range(4)]
    s_xnb = [sp(f"s_xnb{i}", [128, D], BF16) for i in range(4)]
    s_xnT = [sp(f"s_xnT{i}", [128, 4096], BF16) for i in range(2)]
    s_scr = sp("s_scr", [128, D], BF16)
    s_st4 = [sp(f"s_st4{i}", [128, 36], F32) for i in range(2)]
    s_silu = sp("s_silu", [128, HF], F32)
    s_ygs = sp("s_ygs", [128, 4 * HF], BF16)
    s_ybf = [sp(f"s_ybf{i}", [128, HF], BF16) for i in range(2)]
    s_y2 = [sp(f"s_y2{i}", [128, HF], BF16) for i in range(2)]
    s_smu = sp("s_smu", [1, HF], F32)
    s_sex2 = sp("s_sex2", [1, HF], F32)
    s_stmp = sp("s_stmp", [1, HF], F32)
    s_srin = sp("s_srin", [1, HF], F32)
    s_srst = sp("s_srst", [1, HF], F32)
    s_nmu = sp("s_nmu", [1, HF], BF16)
    s_rpk = sp("s_rpk", [16, HF], BF16)
    s_rcol = [sp(f"s_rcol{i}", [128, 16], BF16) for i in range(2)]
    s_rcf = [sp(f"s_rcf{i}", [128, 1], F32) for i in range(2)]
    s_qst = sp("s_qst", [128, 4096], BF16)
    s_dmy = sp("s_dmy", [128, 512], BF16)
    s_kst = [sp(f"s_kst{i}", [128, D], BF16) for i in range(2)]
    s_vst = [sp(f"s_vst{i}", [128, D], BF16) for i in range(2)]
    s_xps = [sp(f"s_xps{i}", [128, D], F32) for i in range(2)]
    s_sx = sp("s_sx", [32, D], F32)

    ap_xs = C.ti["xs"].ap()

    xt_ld = {}
    xn_ve = {}
    xnb_tr = {}
    xnT_tok = {}
    xsum_te = {}
    p1_ac, p2_ve, p3_ac = {}, {}, {}
    xnT_te = {}
    qst_gp = {}
    kv_gp = {}
    xp_gp = {}
    xp_ve = {}
    xp23 = {}

    ve.wait(nc.scalar, C.ve_b1bc)
    ld.wait(nc.scalar, C.ld_consts)
    ve.wait(nc.tensor, C.ve_b1bc)

    def emit_loads(b):
        for r in range(4):
            if b >= 2:
                ve.wait(nc.sync, xn_ve[(b - 1, r)])
            xt_ld[(b, r)] = ld.inc(nc.sync.dma_start(out=s_xt[r][:], in_=ap_xs[r, b]))

    def emit_p1(b):
        sl = b % 2
        st4 = s_st4[sl]
        if b >= 2:
            ve.wait(nc.scalar, xn_ve[(b - 2, 3)])
        for r in range(4):
            ld.wait(nc.scalar, xt_ld[(b, r)])
            _age_sc(C)
            nc.scalar.activation(out=s_scr[:], in_=s_xt[r][:], func=AF.Copy,
                                 accum_out=st4[:, r: r + 1])
            a = nc.scalar.activation(out=s_scr[:], in_=s_xt[r][:], func=AF.Square,
                                     accum_out=st4[:, 4 + r: 5 + r])
        ac.inc(a)
        p1_ac[b] = ac.n

    def emit_p2(b):
        sl = b % 2
        st4 = s_st4[sl]
        ac.wait(nc.vector, p1_ac[b])
        _vh(C, nc.vector.tensor_scalar(out=st4[:, 8:16], in0=st4[:, 0:8],
                                       scalar1=1.0 / D, scalar2=None, op0=ALU.mult))
        _vh(C, nc.vector.tensor_tensor(out=st4[:, 16:20], in0=st4[:, 8:12],
                                       in1=st4[:, 8:12], op=ALU.mult))
        _vh(C, nc.vector.tensor_tensor(out=st4[:, 20:24], in0=st4[:, 12:16],
                                       in1=st4[:, 16:20], op=ALU.subtract))
        _vh(C, nc.vector.tensor_scalar(out=st4[:, 24:28], in0=st4[:, 20:24],
                                       scalar1=1e-5, scalar2=None, op0=ALU.add))
        ve.inc(nc.vector.reciprocal(out=st4[:, 28:32], in_=st4[:, 24:28]))
        p2_ve[b] = ve.n

    def emit_p3(b):
        sl = b % 2
        st4 = s_st4[sl]
        ve.wait(nc.scalar, p2_ve[b])
        ac.inc(nc.scalar.activation(out=st4[:, 32:36], in_=st4[:, 28:32], func=AF.Sqrt))
        p3_ac[b] = ac.n

    def emit_xn(b):
        sl = b % 2
        st4 = s_st4[sl]
        ac.wait(nc.vector, p3_ac[b])
        if b >= 2:
            te.wait(nc.scalar, xnT_te[b - 2])
            te.wait(nc.sync, xnT_te[b - 2])
        for r in range(4):
            ld.wait(nc.vector, xt_ld[(b, r)])
            if b >= 1:
                te.wait(nc.vector, xsum_te[(b - 1, r)])
                for q in xnb_tr[(b - 1, r)]:
                    q[0].wait(nc.vector, q[1])
            ve.inc(nc.vector.tensor_scalar(out=s_xnb[r][:], in0=s_xt[r][:],
                                           scalar1=st4[:, 8 + r: 9 + r],
                                           scalar2=st4[:, 32 + r: 33 + r],
                                           op0=ALU.subtract, op1=ALU.mult))
            xn_ve[(b, r)] = ve.n
        ve.inc(nc.vector.memset(s_dmy[:], 0.0))
        xn_aged = ve.n
        ve.wait(nc.scalar, xn_aged)
        ve.wait(nc.sync, xn_aged)
        toks = []
        for r in range(4):
            rtoks = []
            for h in range(8):
                eng, qs = (nc.scalar, sc) if h % 2 == 0 else (nc.sync, ld)
                rtoks.append((qs, qs.inc(eng.dma_start(
                    out=s_xnT[sl][:, h * 512 + r * 128: h * 512 + r * 128 + 128],
                    in_=s_xnb[r][:, h * 128: h * 128 + 128], transpose=True))))
            xnb_tr[(b, r)] = rtoks
            toks.extend(rtoks)
        xnT_tok[b] = toks

    emit_loads(0)
    emit_p1(0)
    emit_p2(0)
    emit_p3(0)
    emit_xn(0)
    emit_loads(1)
    emit_p1(1)

    for b in range(B):
        bsl = b % 2

        for r in range(4):
            i = b * 4 + r
            ve.wait(nc.tensor, xn_ve[(b, r)])
            nc.tensor.matmul(pk(6, 512, 32), C.s_sel128[:, i * 32: i * 32 + 32],
                             s_xnb[r][:, 0:512],
                             start=(i == 0), stop=(i == 31), skip_group_check=True)
            te.inc(nc.tensor.matmul(pk(7, 512, 32), C.s_sel128[:, i * 32: i * 32 + 32],
                                    s_xnb[r][:, 512:1024],
                                    start=(i == 0), stop=(i == 31),
                                    skip_group_check=True))
            xsum_te[(b, r)] = te.n

        for q in xnT_tok[b]:
            q[0].wait(nc.tensor, q[1])
        if b >= 1:
            gp.wait(nc.vector, qst_gp[b - 1])
        qev = {}
        for q in range(8):
            bank = q % 2
            if q >= 2:
                ve.wait(nc.tensor, qev[q - 2])
            for h in range(8):
                mm = nc.tensor.matmul(
                    pk(bank), C.s_wq[:, h * D + q * 128: h * D + q * 128 + 128],
                    s_xnT[bsl][:, h * 512: h * 512 + 512],
                    start=(h == 0), stop=(h == 7))
            te.inc(mm)
            te.wait(nc.vector)
            ve.inc(nc.vector.tensor_copy(out=s_qst[:, q * 512: q * 512 + 512],
                                         in_=pk(bank)))
            qev[q] = ve.n
        qT_evacs_ve = ve.n
        ve.wait(nc.gpsimd)
        qst_gp[b] = gp.inc(nc.gpsimd.dma_start(out=C.i_qT.ap()[b], in_=s_qst[:]))

        if b + 1 < B:
            emit_p2(b + 1)

        kv_ac = 0
        kev, vev = 0, 0
        if b >= 1:
            ve.wait(nc.tensor, xp23[b - 1][0])
        for r in range(4):
            ksl = r % 2
            if r >= 2:
                gp.wait(nc.scalar, kv_gp[(b, r - 2)])
            elif b >= 1:
                gp.wait(nc.scalar, kv_gp[(b - 1, r + 2)])
            lhs = [s_xnT[bsl][:, h * 512 + r * 128: h * 512 + r * 128 + 128]
                   for h in range(8)]
            if kev:
                ac.wait(nc.tensor, kev)
            for h in range(8):
                nc.tensor.matmul(pk(2), lhs[h], C.s_wk[:, h * D: h * D + 512],
                                 start=(h == 0), stop=(h == 7))
                mm = nc.tensor.matmul(pk(3), lhs[h], C.s_wk[:, h * D + 512: h * D + 1024],
                                      start=(h == 0), stop=(h == 7))
            te.inc(mm)
            te.wait(nc.scalar)
            ac.inc(nc.scalar.activation(out=s_kst[ksl][:], in_=psum[:, 1024:2048],
                                        func=AF.Copy))
            kev = ac.n
            if r == 0 and b >= 1:
                ve.wait(nc.tensor, xp23[b - 1][1])
            if vev:
                ac.wait(nc.tensor, vev)
            for h in range(8):
                nc.tensor.matmul(pk(4), lhs[h], C.s_wv[:, h * D: h * D + 512],
                                 start=(h == 0), stop=(h == 7))
                mm = nc.tensor.matmul(pk(5), lhs[h], C.s_wv[:, h * D + 512: h * D + 1024],
                                      start=(h == 0), stop=(h == 7))
            te.inc(mm)
            te.wait(nc.scalar)
            ac.inc(nc.scalar.activation(out=s_vst[ksl][:], in_=psum[:, 2048:3072],
                                        func=AF.Copy))
            vev = ac.n
            ac.wait(nc.gpsimd)
            l = 3 - r
            t1 = gp.inc(nc.gpsimd.dma_start(
                out=bass.AP(C.i_kshard, (l * B + b) * 128 * D, [[D, 128], [1, D]]),
                in_=s_kst[ksl][:]))
            t2 = gp.inc(nc.gpsimd.dma_start(
                out=bass.AP(C.i_vshard, (l * B + b) * 128 * D, [[D, 128], [1, D]]),
                in_=s_vst[ksl][:]))
            kv_gp[(b, r)] = [t1, t2]
        kv_ac = vev

        if b + 1 < B:
            emit_p3(b + 1)
            emit_xn(b + 1)

        ac.wait(nc.tensor, kv_ac)
        y2_ac = {}
        ygs_ve = {}
        silu_ac = {}

        def emit_ones(f2):
            ac.wait(nc.tensor, y2_ac[f2])
            nc.tensor.matmul(pk(0, 512, 1), C.s_onesc[:, 0:1], s_ybf[f2 % 2][:],
                             start=(f2 == 0), stop=(f2 == 3))
            mm = nc.tensor.matmul(pk(1, 512, 1), C.s_onesc[:, 0:1], s_y2[f2 % 2][:],
                                  start=(f2 == 0), stop=(f2 == 3))
            te.inc(mm)

        for f in range(4):
            bm = 2 + 2 * (f % 2)
            if f == 0:
                ve.wait(nc.tensor, qT_evacs_ve)
            if f >= 2:
                ve.wait(nc.tensor, ygs_ve[f - 2])
                ac.wait(nc.tensor, silu_ac[f - 2])
            for h in range(8):
                rhs = s_xnT[bsl][:, h * 512: h * 512 + 512]
                nc.tensor.matmul(pk(bm), C.s_sw1[:, h * HF + f * 128: h * HF + f * 128 + 128],
                                 rhs, start=(h == 0), stop=(h == 7))
                mm = nc.tensor.matmul(pk(bm + 1), C.s_sw2[:, h * HF + f * 128: h * HF + f * 128 + 128],
                                      rhs, start=(h == 0), stop=(h == 7))
            te.inc(mm)
            mids_te = te.n
            if f >= 1:
                emit_ones(f - 1)
            te.wait(nc.scalar, mids_te)
            ac.inc(nc.scalar.activation(out=s_silu[:], in_=pk(bm), func=AF.Silu,
                                        bias=C.s_cols[:, 0 + f:1 + f]))
            silu_ac[f] = ac.n
            ac.wait(nc.vector)
            nc.vector.scalar_tensor_tensor(out=s_ybf[f % 2][:], in0=pk(bm + 1),
                                           scalar=C.s_cols[:, 4 + f:5 + f], in1=s_silu[:],
                                           op0=ALU.add, op1=ALU.mult)
            ve.inc(nc.vector.tensor_scalar(out=s_ygs[:, f * HF: f * HF + HF],
                                           in0=s_ybf[f % 2][:],
                                           scalar1=C.s_cols[:, 8 + f:9 + f],
                                           scalar2=None, op0=ALU.mult))
            ygs_ve[f] = ve.n
            ve.wait(nc.scalar)
            ac.inc(nc.scalar.activation(out=s_y2[f % 2][:], in_=s_ybf[f % 2][:],
                                        func=AF.Square))
            y2_ac[f] = ac.n
        emit_ones(3)
        ones_te = te.n
        te.wait(nc.vector, ones_te)
        _age_ve(C)
        nc.vector.tensor_scalar(out=s_smu[:], in0=psum[0:1, 0:512],
                                scalar1=1.0 / HF, scalar2=None, op0=ALU.mult)
        nc.vector.tensor_scalar(out=s_sex2[:], in0=psum[0:1, 512:1024],
                                scalar1=1.0 / HF, scalar2=None, op0=ALU.mult)
        nc.vector.tensor_tensor(out=s_stmp[:], in0=s_smu[:], in1=s_smu[:], op=ALU.mult)
        nc.vector.tensor_tensor(out=s_stmp[:], in0=s_sex2[:], in1=s_stmp[:],
                                op=ALU.subtract)
        nc.vector.tensor_scalar(out=s_stmp[:], in0=s_stmp[:],
                                scalar1=1e-5, scalar2=None, op0=ALU.add)
        ve.inc(nc.vector.reciprocal(out=s_srin[:], in_=s_stmp[:]))
        stats_ev = ve.n
        ve.wait(nc.scalar, stats_ev)
        nc.scalar.activation(out=s_srst[:], in_=s_srin[:], func=AF.Sqrt)
        nc.scalar.activation(out=s_nmu[:], in_=s_smu[:], func=AF.Copy, scale=-1.0)
        ac.inc(nc.scalar.activation(out=s_rpk[0:1, :], in_=s_srst[:], func=AF.Copy))
        rows_ac = ac.n
        for t in range(4):
            gi = b * 4 + t
            bA = 2 + 2 * (t % 2)
            if gi >= 2:
                ve.wait(nc.scalar, xp_ve[gi - 2])
            ac.wait(nc.scalar, rows_ac)
            rc_sc = sc.inc(nc.scalar.dma_start(out=s_rcol[t % 2][:],
                                               in_=s_rpk[:, t * 128: t * 128 + 128],
                                               transpose=True))
            if t < 2:
                ac.wait(nc.tensor, silu_ac[2 + t])
                ve.wait(nc.tensor, ygs_ve[2 + t])
                if t == 0:
                    ve.wait(nc.tensor, stats_ev)
            else:
                ve.wait(nc.tensor, xp_ve[gi - 2])
            for f in range(4):
                nc.tensor.matmul(pk(bA), s_ygs[:, f * HF + t * 128: f * HF + t * 128 + 128],
                                 C.s_sw3[:, f * D: f * D + 512],
                                 start=(f == 0), stop=False)
                nc.tensor.matmul(pk(bA + 1), s_ygs[:, f * HF + t * 128: f * HF + t * 128 + 128],
                                 C.s_sw3[:, f * D + 512: f * D + 1024],
                                 start=(f == 0), stop=False)
            nc.tensor.matmul(pk(bA), s_nmu[0:1, t * 128: t * 128 + 128],
                             C.s_G1b[0:1, 0:512], start=False, stop=True)
            mm = nc.tensor.matmul(pk(bA + 1), s_nmu[0:1, t * 128: t * 128 + 128],
                                  C.s_G1b[0:1, 512:1024], start=False, stop=True)
            te.inc(mm)
            a_te = te.n
            if gi >= 2:
                gp.wait(nc.sync, xp_gp[gi - 2])
            xp_ld = ld.inc(nc.sync.dma_start(out=s_xps[t % 2][:], in_=ap_xs[t, b]))
            te.wait(nc.vector, a_te)
            sc.wait(nc.vector, rc_sc)
            ld.wait(nc.vector, xp_ld)
            _vh(C, nc.vector.tensor_copy(out=s_rcf[t % 2][:], in_=s_rcol[t % 2][:, 0:1]))
            nc.vector.tensor_tensor(out=s_xps[t % 2][:], in0=s_xps[t % 2][:],
                                    in1=C.s_B1bc[:], op=ALU.add)
            nc.vector.scalar_tensor_tensor(out=s_xps[t % 2][:],
                                           in0=psum[:, bA * 512: bA * 512 + 1024],
                                           scalar=s_rcf[t % 2][:, 0:1],
                                           in1=s_xps[t % 2][:],
                                           op0=ALU.mult, op1=ALU.add)
            ve.inc(nc.vector.memset(s_dmy[:, 0:64], 0.0))
            xp_ve[gi] = ve.n
            ve.wait(nc.gpsimd)
            xp_gp[gi] = gp.inc(nc.gpsimd.dma_start(out=C.i_xp.ap()[t, b],
                                                   in_=s_xps[t % 2][:]))
        xnT_te[b] = te.n
        xp23[b] = (xp_ve[b * 4 + 2], xp_ve[b * 4 + 3])

        if b + 2 < B:
            emit_loads(b + 2)
            emit_p1(b + 2)

    te.wait(nc.vector)
    _age_ve(C)
    nc.vector.tensor_copy(out=C.s_xsumr[:], in_=psum[0:32, 3072:4096])
    ve.inc(nc.vector.memset(s_dmy[:], 0.0))
    ve.wait(nc.scalar)
    xsT_toks = []
    for h in range(8):
        xsT_toks.append(sc.inc(nc.scalar.dma_start(
            out=C.s_xsumT[:, h * 32: h * 32 + 32],
            in_=C.s_xsumr[:, h * 128: h * 128 + 128], transpose=True)))
    sc.wait(nc.tensor, xsT_toks)
    ve.wait(nc.tensor)
    for h in range(8):
        nc.tensor.matmul(pk(0, 512, 32), C.s_xsumT[:, h * 32: h * 32 + 32],
                         C.s_wv[:, h * D: h * D + 512], start=(h == 0), stop=(h == 7))
        mm = nc.tensor.matmul(pk(1, 512, 32), C.s_xsumT[:, h * 32: h * 32 + 32],
                              C.s_wv[:, h * D + 512: h * D + 1024],
                              start=(h == 0), stop=(h == 7))
    te.inc(mm)
    te.wait(nc.vector)
    nc.vector.tensor_copy(out=s_sx[:], in_=psum[0:32, 0:1024])
    ve.inc(nc.vector.memset(s_dmy[:, 0:64], 0.0))
    ve.wait(nc.gpsimd)
    gp.inc(nc.gpsimd.dma_start(out=C.i_sshard.ap(), in_=s_sx[:]))
    C.p1_gp = gp.snap()
    C.p1_te = te.n
    C.p1_ve = ve.n
    p1.close()
    C.wscope.close()


def _emit_collectives(C):
    nc = C.nc
    gp, coll = C.sem["gp"], C.sem["coll"]
    rg = [list(range(NCORE))]
    gp.wait(nc.gpsimd, C.p1_gp)
    coll.inc(nc.gpsimd.collective_compute("AllGather", ALU.bypass,
                                          ins=[C.i_kshard.ap()],
                                          outs=[bass.AP(C.i_gb, C.kg_off,
                                                        [[D, NCORE * 4 * B * 128], [1, D]])],
                                          replica_groups=rg))
    coll.inc(nc.gpsimd.collective_compute("AllReduce", ALU.add,
                                          ins=[C.i_fl.ap()], outs=[C.i_flo.ap()],
                                          replica_groups=rg))
    coll.inc(nc.gpsimd.collective_compute("AllGather", ALU.bypass,
                                          ins=[C.i_vshard.ap()],
                                          outs=[bass.AP(C.i_gb, C.vg_off,
                                                        [[D, NCORE * 4 * B * 128], [1, D]])],
                                          replica_groups=rg))
    coll.inc(nc.gpsimd.collective_compute("AllGather", ALU.bypass,
                                          ins=[C.i_sshard.ap()], outs=[C.i_sg.ap()],
                                          replica_groups=rg))
    coll.inc(nc.gpsimd.collective_compute("AllReduce", ALU.add,
                                          ins=[C.i_fl.ap()], outs=[C.i_flo.ap()],
                                          replica_groups=rg))
    C.coll_k, C.coll_v, C.coll_s = 2, 5, 5

    p2a = ExitStack()

    def sa(name, shape, dtype):
        return p2a.enter_context(nc.sbuf_tensor(name, shape, dtype))

    C.s_denom = sa("s_denom", [128, 8448], F32)
    C.sem["ve"].inc(nc.vector.memset(C.s_denom[:], 0.0))
    C.s_ew1 = [sa(f"s_ew1{i}", [128, 8 * HF], BF16) for i in range(2)]
    C.s_ew2 = [sa(f"s_ew2{i}", [128, 8 * HF], BF16) for i in range(2)]
    C.s_ew3 = [sa(f"s_ew3{i}", [128, 4 * D], BF16) for i in range(2)]

    C.ew_ld0 = []
    for e in range(2):
        C.ew_ld0.append(C.wload(C.s_ew1[e], C.ti["ew1T"], HF, 8, e))
        C.ew_ld0.append(C.wload(C.s_ew2[e], C.ti["ew2T"], HF, 8, e))
        C.ew_ld0.append(C.wload(C.s_ew3[e], C.ti["ew3T"], D, 4, e))

    p2b = ExitStack()
    C.s_ka = [p2b.enter_context(nc.sbuf_tensor(f"s_ka{i}", [128, 8192], BF16))
              for i in range(2)]
    C.s_est = [p2b.enter_context(nc.sbuf_tensor(f"s_est{i}", [128, 4096], BF16))
               for i in range(2)]
    C.s_qtb = [p2b.enter_context(nc.sbuf_tensor(f"s_qtb{i}", [128, 4096], BF16))
               for i in range(2)]
    p2b.close()

    p2c = ExitStack()

    def sm(name, shape, dtype):
        return p2c.enter_context(nc.sbuf_tensor(name, shape, dtype))

    C.s_xmt = [sm(f"s_xmt{i}", [128, D], F32) for i in range(2)]
    C.s_xmb = sm("s_xmb", [128, D], BF16)
    C.s_xmT = sm("s_xmT", [128, 8 * 256], BF16)
    C.s_mscr = sm("s_mscr", [128, D], BF16)
    C.s_mst = sm("s_mst", [128, 10], F32)
    C.s_g9 = [sm(f"s_g9{i}", [128, 9], F32) for i in range(2)]
    C.s_e8 = sm("s_e8", [128, 8], F32)
    C.s_m8 = sm("s_m8", [128, 8], F32)
    C.s_oh = [sm(f"s_oh{i}", [128, 8], F32) for i in range(2)]
    C.s_w2 = sm("s_w2", [128, 2], F32)
    C.s_w2e = sm("s_w2e", [128, 2], F32)
    C.s_wex = [sm(f"s_wex{i}", [128, 1], F32) for i in range(2)]
    C.s_wshm = [sm(f"s_wshm{i}", [128, 1], F32) for i in range(2)]
    C.s_msil = sm("s_msil", [128, 256], F32)
    C.s_mybf = sm("s_mybf", [128, 256], BF16)
    C.s_my2 = sm("s_my2", [128, 256], BF16)
    C.s_mygs = sm("s_mygs", [128, 4 * 256], BF16)
    C.s_mmu = sm("s_mmu", [1, 256], F32)
    C.s_mex2 = sm("s_mex2", [1, 256], F32)
    C.s_mtmp = sm("s_mtmp", [1, 256], F32)
    C.s_mrin = sm("s_mrin", [1, 256], F32)
    C.s_mrst = sm("s_mrst", [1, 256], F32)
    C.s_selbE = sm("s_selbE", [128, 32], BF16)
    C.s_mnmu = sm("s_mnmu", [1, 256], BF16)
    C.s_mrpk = sm("s_mrpk", [16, 256], BF16)
    C.s_mrc = [sm(f"s_mrc{i}", [128, 16], BF16) for i in range(2)]
    C.s_mrcf = [sm(f"s_mrcf{i}", [128, 1], F32) for i in range(2)]
    C.s_sm = [sm(f"s_sm{i}", [128, D], F32) for i in range(2)]
    C.s_moeA = [sm(f"s_moeA{i}", [128, D], F32) for i in range(2)]
    C.s_acc = [sm(f"s_acc{i}", [128, D], F32) for i in range(2)]
    C.s_b1e = sm("s_b1e", [8, D], BF16)
    C.s_g1e = sm("s_g1e", [1, D], BF16)
    C.s_ohew = [sm(f"s_ohew{i}", [128, 8], F32) for i in range(2)]
    C.s_pk128 = sm("s_pk128", [128, 128], BF16)
    C.s_pkT = [sm(f"s_pkT{i}", [128, 128], BF16) for i in range(2)]
    C.s_sc1 = sm("s_sc1", [128, 1], F32)
    p2c.close()

    p2d = ExitStack()

    def sd(name, shape, dtype):
        return p2d.enter_context(nc.sbuf_tensor(name, shape, dtype))

    C.s_erd = [sd(f"s_erd{i}", [128, 4096], BF16) for i in range(2)]
    C.s_vr = [sd(f"s_vr{i}", [128, D], BF16) for i in range(4)]
    C.s_att = [sd(f"s_att{i}", [128, 512], BF16) for i in range(2)]
    C.s_x2p = [sd(f"s_x2p{i}", [128, D], F32) for i in range(2)]
    C.s_ost = [sd(f"s_ost{i}", [128, D], F32) for i in range(2)]
    C.s_sb2 = sd("s_sb2", [32, D], BF16)
    p2d.close()
    p2a.close()


def _age_ve(C):
    C.sem["ve"].inc(C.nc.vector.memset(C.s_agev[:], 0.0))
    return C.sem["ve"].n


def _age_te(C):
    return C.nc.tensor.matmul(C.psum[64:65, 2048:2080], C.s_sel128[:, 0:1],
                              C.s_sel128[:, 0:32], start=True, stop=True,
                              skip_group_check=True)


def _age_sc(C):
    C.sem["ac"].inc(C.nc.scalar.activation(out=C.s_ages[:], in_=C.s_cols[:, 0:16],
                                           func=AF.Copy))
    return C.sem["ac"].n


def _vh(C, inst):
    ve = C.sem["ve"]
    ve.inc(inst)
    ve.wait(C.nc.vector)


def _cum_of(c):
    SR = _strips_desc(c)
    umax = SR[0]

    def nb_of(u):
        return sum(1 for s in SR if s >= u)

    cum = [0]
    for u in range(32):
        cum.append(cum[-1] + (nb_of(u) * 128 if u <= umax else 0))
    return SR, umax, nb_of, cum


def _emit_2A(C, c):
    nc = C.nc
    ld, sc, gp, te, ve, ac = (C.sem[n] for n in ["ld", "sc", "gp", "te", "ve", "ac"])
    pk = C.pk
    SR, umax, nb_of, cum = _cum_of(c)
    amax = umax // 8
    coll = C.sem["coll"]

    coll.wait(nc.sync, C.coll_k)
    ve.wait(nc.tensor, C.p1_ve)
    if C.dbg:
        coll.wait(nc.gpsimd, C.coll_k)
        st = C.sem["st"]
        for cc in range(NCORE):
            st.inc(nc.gpsimd.dma_start(
                out=bass.AP(C.d_kg, cc * 1024 * D, [[D, 1024], [1, D]]),
                in_=bass.AP(C.i_gb, C.kg_off + cc * 4096 * D, [[D, 1024], [1, D]])))
    s_ka, s_est, s_qtb = C.s_ka, C.s_est, C.s_qtb
    qtb_ld, qtb_te, ka_ld, ka_te, est_sc = {}, {}, {}, {}, {}
    bank_ac = {}
    es = D
    bs = 8 * 128 * es
    for b in range(B):
        if b >= 2:
            te.wait(nc.sync, qtb_te[b - 2])
        qtb_ld[b] = ld.inc(nc.sync.dma_start(out=s_qtb[b % 2][:], in_=C.i_qT.ap()[b]))
        for a in range(amax + 1):
            g = b * (amax + 1) + a
            sl = g % 2
            if g >= 2:
                te.wait(nc.sync, ka_te[g - 2])
            katoks = []
            for part0, base_blk, qsgn in ((0, 0, 1), (64, 4, 1), (0, 30, -1), (64, 26, -1)):
                qstride = qsgn * 8 * bs
                colbase = 0 if qsgn == 1 else 4096
                for p2 in range(2):
                    off = ((base_blk + p2) * 8 + b) * 128 * es + a * es
                    dst = s_ka[sl][part0 + 32 * p2: part0 + 32 * p2 + 32,
                                   colbase:colbase + 4096]
                    src = bass.AP(C.i_gb, C.kg_off + off,
                                  [[4 * es, 32], [qstride, 4], [1, es]])
                    katoks.append(ld.inc(nc.sync.dma_start(
                        out=dst.rearrange("p (q n) -> p q n", q=4), in_=src)))
            ka_ld[g] = katoks
            ld.wait(nc.scalar, katoks)
            if a == 0:
                ld.wait(nc.scalar, qtb_ld[b])
            ka_relay = _age_sc(C)
            u_hi = min(8 * a + 7, umax)
            mm_te = {}

            def emit_exp(u2):
                nb2 = nb_of(u2)
                bank2 = u2 % 4
                off2 = cum[u2] - cum[8 * a]
                te.wait(nc.scalar, mm_te[u2])
                _age_sc(C)
                ac.inc(nc.scalar.activation(
                    out=s_est[sl][:, off2: off2 + nb2 * 128],
                    in_=pk(bank2, nb2 * 128), func=AF.Exp, scale=1.0 / 32.0))
                bank_ac[bank2] = ac.n
                ac.wait(nc.vector)
                ve.inc(nc.vector.tensor_tensor(
                    out=C.s_denom[:, cum[u2]: cum[u2] + nb2 * 128],
                    in0=C.s_denom[:, cum[u2]: cum[u2] + nb2 * 128],
                    in1=s_est[sl][:, off2: off2 + nb2 * 128], op=ALU.add))

            pend = None
            for u in range(8 * a, u_hi + 1):
                nb = nb_of(u)
                bank = u % 4
                if bank in bank_ac:
                    ac.wait(nc.tensor, bank_ac[bank])
                if u == 8 * a:
                    ld.wait(nc.tensor, ka_ld[g])
                    ld.wait(nc.tensor, qtb_ld[b])
                    ac.wait(nc.tensor, ka_relay)
                    if g >= 2:
                        sc.wait(nc.scalar, est_sc[g - 2])
                for q in range(8):
                    mm = nc.tensor.matmul(
                        pk(bank, nb * 128),
                        s_ka[sl][:, q * 1024 + (u % 8) * 128: q * 1024 + (u % 8) * 128 + 128],
                        s_qtb[b % 2][:, q * 512: q * 512 + nb * 128],
                        start=(q == 0), stop=(q == 7))
                if u <= 8 * a + 2:
                    mm = _age_te(C)
                te.inc(mm)
                mm_te[u] = te.n
                if pend is not None:
                    emit_exp(pend)
                pend = u
            emit_exp(pend)
            ka_te[g] = te.n
            W = cum[u_hi + 1] - cum[8 * a]
            ve.wait(nc.scalar)
            est_sc[g] = sc.inc(nc.scalar.dma_start(
                out=C.i_E.ap()[b][:, cum[8 * a]: cum[8 * a] + W],
                in_=s_est[sl][:, 0:W]))
        qtb_te[b] = te.n
    C.a2_sc = sc.snap()
    C.a2_te = te.n
    C.a2_ve = ve.n
    C.a2_ac = ac.n


def _emit_moe(C):
    nc = C.nc
    ld, sc, gp, te, ve, ac = (C.sem[n] for n in ["ld", "sc", "gp", "te", "ve", "ac"])
    pk = C.pk
    psum = C.psum
    ap_xm = C.ti["xmoe"].ap()

    te.wait(nc.sync, C.a2_te)
    sc.wait(nc.sync, C.a2_sc)
    te.wait(nc.scalar, C.a2_te)
    sc.wait(nc.scalar, C.a2_sc)
    te.wait(nc.vector, C.a2_te)
    sc.wait(nc.vector, C.a2_sc)
    te.wait(nc.gpsimd, C.a2_te)
    sc.wait(nc.gpsimd, C.a2_sc)

    xmT_sc = []
    for t in range(2):
        xmtok = ld.inc(nc.sync.dma_start(out=C.s_xmt[t][:],
                                         in_=ap_xm[t * 128:(t + 1) * 128, :]))
        ld.wait(nc.scalar, xmtok)
        _age_sc(C)
        if t == 1:
            ve.wait(nc.scalar)
        nc.scalar.activation(out=C.s_mscr[:], in_=C.s_xmt[t][:], func=AF.Copy,
                             accum_out=C.s_mst[:, 0:1])
        ac.inc(nc.scalar.activation(out=C.s_mscr[:], in_=C.s_xmt[t][:], func=AF.Square,
                                    accum_out=C.s_mst[:, 1:2]))
        ac.wait(nc.vector)
        m = C.s_mst
        _vh(C, nc.vector.tensor_scalar(out=m[:, 2:3], in0=m[:, 0:1], scalar1=1.0 / D,
                                       scalar2=None, op0=ALU.mult))
        _vh(C, nc.vector.tensor_scalar(out=m[:, 3:4], in0=m[:, 1:2], scalar1=1.0 / D,
                                       scalar2=None, op0=ALU.mult))
        _vh(C, nc.vector.tensor_tensor(out=m[:, 4:5], in0=m[:, 2:3], in1=m[:, 2:3],
                                       op=ALU.mult))
        _vh(C, nc.vector.tensor_tensor(out=m[:, 5:6], in0=m[:, 3:4], in1=m[:, 4:5],
                                       op=ALU.subtract))
        _vh(C, nc.vector.tensor_scalar(out=m[:, 6:7], in0=m[:, 5:6], scalar1=1e-5,
                                       scalar2=None, op0=ALU.add))
        ve.inc(nc.vector.reciprocal(out=m[:, 7:8], in_=m[:, 6:7]))
        ve.wait(nc.scalar)
        ac.inc(nc.scalar.activation(out=m[:, 8:9], in_=m[:, 7:8], func=AF.Sqrt))
        ac.wait(nc.vector)
        if t == 1:
            sc.wait(nc.vector, xmT_sc)
        nc.vector.tensor_scalar(out=C.s_xmb[:], in0=C.s_xmt[t][:],
                                scalar1=m[:, 2:3], scalar2=m[:, 8:9],
                                op0=ALU.subtract, op1=ALU.mult)
        _age_ve(C)
        ve.wait(nc.scalar)
        xmT_sc = []
        for h in range(8):
            xmT_sc.append(sc.inc(nc.scalar.dma_start(
                out=C.s_xmT[:, h * 256 + t * 128: h * 256 + t * 128 + 128],
                in_=C.s_xmb[:, h * 128: h * 128 + 128], transpose=True)))

    sc.wait(nc.tensor, sc.snap())
    ac.wait(nc.tensor, C.a2_ac)
    for t in range(2):
        for h in range(8):
            mm = nc.tensor.matmul(
                pk(t, 9), C.s_xmT[:, h * 256 + t * 128: h * 256 + t * 128 + 128],
                C.s_gw[:, h * 9: h * 9 + 9], start=(h == 0), stop=(h == 7))
        te.inc(_age_te(C))
    te.wait(nc.vector)
    _age_ve(C)
    for t in range(2):
        ve.inc(nc.vector.tensor_tensor(out=C.s_g9[t][:], in0=pk(t, 9),
                                       in1=C.s_gbrep[:], op=ALU.add))
    g_ve = ve.n
    ve.wait(nc.scalar, g_ve)
    for t in range(2):
        ac.inc(nc.scalar.activation(out=C.s_g9[t][:], in_=C.s_g9[t][:], func=AF.Sigmoid))
    sig_ac = ac.n
    ac.wait(nc.vector, sig_ac)
    for t in range(2):
        _vh(C, nc.vector.tensor_tensor(out=C.s_e8[:], in0=C.s_g9[t][:, 1:9],
                                       in1=C.s_ebrep[:], op=ALU.add))
        _vh(C, nc.vector.max(out=C.s_m8[:], in_=C.s_e8[:]))
        _vh(C, nc.vector.tensor_scalar(out=C.s_oh[t][:], in0=C.s_e8[:],
                                       scalar1=C.s_m8[:, 0:1], scalar2=None,
                                       op0=ALU.is_equal))
        _vh(C, nc.vector.tensor_tensor(out=C.s_e8[:], in0=C.s_g9[t][:, 1:9],
                                       in1=C.s_oh[t][:], op=ALU.mult))
        _vh(C, nc.vector.reduce_sum(out=C.s_w2[:, 1:2], in_=C.s_e8[:],
                                    axis=mybir.AxisListType.X))
        ve.inc(nc.vector.tensor_copy(out=C.s_w2[:, 0:1], in_=C.s_g9[t][:, 0:1]))
        ve.wait(nc.scalar)
        ac.inc(nc.scalar.activation(out=C.s_w2e[:], in_=C.s_w2[:], func=AF.Exp))
        ac.wait(nc.vector)
        _vh(C, nc.vector.reduce_sum(out=C.s_sc1[:], in_=C.s_w2e[:],
                                    axis=mybir.AxisListType.X))
        _vh(C, nc.vector.reciprocal(out=C.s_sc1[:], in_=C.s_sc1[:]))
        _vh(C, nc.vector.tensor_tensor(out=C.s_wex[t][:], in0=C.s_w2e[:, 1:2],
                                       in1=C.s_sc1[:], op=ALU.mult))
        _vh(C, nc.vector.tensor_tensor(out=C.s_wshm[t][:], in0=C.s_w2e[:, 0:1],
                                       in1=C.s_sc1[:], op=ALU.mult))
        ve.inc(nc.vector.tensor_scalar(out=C.s_wshm[t][:], in0=C.s_wshm[t][:],
                                       scalar1=-1.0, scalar2=None, op0=ALU.add))
    gates_done = ve.n

    def ffn_T(lhs1, lhs2, bias1, bias2, w3, g1row, out_tiles, add_b1bc):
        for f in range(4):
            ve.wait(nc.tensor)
            ac.wait(nc.tensor)
            for h in range(8):
                nc.tensor.matmul(pk(0, 256),
                                 lhs1[:, h * HF + f * 128: h * HF + f * 128 + 128],
                                 C.s_xmT[:, h * 256: h * 256 + 256],
                                 start=(h == 0), stop=(h == 7))
                mm = nc.tensor.matmul(pk(1, 256),
                                      lhs2[:, h * HF + f * 128: h * HF + f * 128 + 128],
                                      C.s_xmT[:, h * 256: h * 256 + 256],
                                      start=(h == 0), stop=(h == 7))
            te.inc(_age_te(C))
            te.wait(nc.scalar)
            _age_sc(C)
            ac.inc(nc.scalar.activation(out=C.s_msil[:], in_=pk(0, 256), func=AF.Silu,
                                        bias=bias1[f]))
            ac.wait(nc.vector)
            nc.vector.scalar_tensor_tensor(out=C.s_mybf[:], in0=pk(1, 256),
                                           scalar=bias2[f], in1=C.s_msil[:],
                                           op0=ALU.add, op1=ALU.mult)
            ve.inc(nc.vector.tensor_scalar(out=C.s_mygs[:, f * 256: f * 256 + 256],
                                           in0=C.s_mybf[:],
                                           scalar1=C.s_cols[:, 80 + f: 81 + f],
                                           scalar2=None, op0=ALU.mult))
            ve.wait(nc.scalar)
            ac.inc(nc.scalar.activation(out=C.s_my2[:], in_=C.s_mybf[:], func=AF.Square))
            ac.wait(nc.tensor)
            nc.tensor.matmul(pk(4, 256, 1), C.s_onesc[:, 0:1], C.s_mybf[:],
                             start=(f == 0), stop=(f == 3))
            mm = nc.tensor.matmul(pk(5, 256, 1), C.s_onesc[:, 0:1], C.s_my2[:],
                                  start=(f == 0), stop=(f == 3))
            te.inc(mm)
            te.wait(nc.vector)
        _age_ve(C)
        nc.vector.tensor_scalar(out=C.s_mmu[:], in0=C.psum[0:1, 2048:2304],
                                scalar1=1.0 / HF, scalar2=None, op0=ALU.mult)
        nc.vector.tensor_scalar(out=C.s_mex2[:], in0=C.psum[0:1, 2560:2816],
                                scalar1=1.0 / HF, scalar2=None, op0=ALU.mult)
        nc.vector.tensor_tensor(out=C.s_mtmp[:], in0=C.s_mmu[:], in1=C.s_mmu[:],
                                op=ALU.mult)
        nc.vector.tensor_tensor(out=C.s_mtmp[:], in0=C.s_mex2[:], in1=C.s_mtmp[:],
                                op=ALU.subtract)
        nc.vector.tensor_scalar(out=C.s_mtmp[:], in0=C.s_mtmp[:], scalar1=1e-5,
                                scalar2=None, op0=ALU.add)
        ve.inc(nc.vector.reciprocal(out=C.s_mrin[:], in_=C.s_mtmp[:]))
        st_ev = ve.n
        ve.wait(nc.scalar, st_ev)
        nc.scalar.activation(out=C.s_mrst[:], in_=C.s_mrin[:], func=AF.Sqrt)
        nc.scalar.activation(out=C.s_mnmu[:], in_=C.s_mmu[:], func=AF.Copy, scale=-1.0)
        ac.inc(nc.scalar.activation(out=C.s_mrpk[0:1, :], in_=C.s_mrst[:], func=AF.Copy))
        rows_ac = ac.n
        ac.wait(nc.scalar, rows_ac)
        rc_sc = []
        for t in range(2):
            rc_sc.append(sc.inc(nc.scalar.dma_start(
                out=C.s_mrc[t][:], in_=C.s_mrpk[:, t * 128: t * 128 + 128],
                transpose=True)))
        ve.wait(nc.tensor, st_ev)
        a_done_ve = None
        for t in range(2):
            bA, bB = (2, 3)
            if a_done_ve is not None:
                ve.wait(nc.tensor, a_done_ve)
            for f in range(4):
                nc.tensor.matmul(pk(bA),
                                 C.s_mygs[:, f * 256 + t * 128: f * 256 + t * 128 + 128],
                                 w3[:, f * D: f * D + 512], start=(f == 0), stop=False)
                nc.tensor.matmul(pk(bB),
                                 C.s_mygs[:, f * 256 + t * 128: f * 256 + t * 128 + 128],
                                 w3[:, f * D + 512: f * D + 1024],
                                 start=(f == 0), stop=False)
            nc.tensor.matmul(pk(bA), C.s_mnmu[0:1, t * 128: t * 128 + 128],
                             g1row[0:1, 0:512], start=False, stop=True)
            mm = nc.tensor.matmul(pk(bB), C.s_mnmu[0:1, t * 128: t * 128 + 128],
                                  g1row[0:1, 512:1024], start=False, stop=True)
            te.inc(mm)
            a_te = te.n
            if t == 0:
                sc.wait(nc.vector, rc_sc)
            te.wait(nc.vector, a_te)
            _vh(C, nc.vector.tensor_copy(out=C.s_mrcf[t][:], in_=C.s_mrc[t][:, 0:1]))
            if add_b1bc:
                ve.inc(nc.vector.scalar_tensor_tensor(
                    out=out_tiles[t][:], in0=psum[:, 1024:2048],
                    scalar=C.s_mrcf[t][:, 0:1], in1=C.s_B1bc[:],
                    op0=ALU.mult, op1=ALU.add))
            else:
                ve.inc(nc.vector.tensor_scalar(
                    out=out_tiles[t][:], in0=psum[:, 1024:2048],
                    scalar1=C.s_mrcf[t][:, 0:1], scalar2=None, op0=ALU.mult))
            a_done_ve = ve.n
        out_ve = ve.n
        ve.wait(nc.tensor, out_ve)
        return out_ve

    sh1 = [C.s_cols[:, 0 + f:1 + f] for f in range(4)]
    sh2 = [C.s_cols[:, 4 + f:5 + f] for f in range(4)]

    def ffn_T_shared():
        for f in range(4):
            ve.wait(nc.tensor)
            ac.wait(nc.tensor)
            for h in range(8):
                nc.tensor.matmul(pk(0, 256),
                                 C.s_sw1[:, h * HF + f * 128: h * HF + f * 128 + 128],
                                 C.s_xmT[:, h * 256: h * 256 + 256],
                                 start=(h == 0), stop=(h == 7))
                mm = nc.tensor.matmul(pk(1, 256),
                                      C.s_sw2[:, h * HF + f * 128: h * HF + f * 128 + 128],
                                      C.s_xmT[:, h * 256: h * 256 + 256],
                                      start=(h == 0), stop=(h == 7))
            te.inc(_age_te(C))
            te.wait(nc.scalar)
            _age_sc(C)
            ac.inc(nc.scalar.activation(out=C.s_msil[:], in_=pk(0, 256), func=AF.Silu,
                                        bias=sh1[f]))
            ac.wait(nc.vector)
            nc.vector.scalar_tensor_tensor(out=C.s_mybf[:], in0=pk(1, 256),
                                           scalar=sh2[f], in1=C.s_msil[:],
                                           op0=ALU.add, op1=ALU.mult)
            ve.inc(nc.vector.tensor_scalar(out=C.s_mygs[:, f * 256: f * 256 + 256],
                                           in0=C.s_mybf[:],
                                           scalar1=C.s_cols[:, 8 + f: 9 + f],
                                           scalar2=None, op0=ALU.mult))
            ve.wait(nc.scalar)
            ac.inc(nc.scalar.activation(out=C.s_my2[:], in_=C.s_mybf[:], func=AF.Square))
            ac.wait(nc.tensor)
            nc.tensor.matmul(pk(4, 256, 1), C.s_onesc[:, 0:1], C.s_mybf[:],
                             start=(f == 0), stop=(f == 3))
            mm = nc.tensor.matmul(pk(5, 256, 1), C.s_onesc[:, 0:1], C.s_my2[:],
                                  start=(f == 0), stop=(f == 3))
            te.inc(mm)
            te.wait(nc.vector)
        _age_ve(C)
        nc.vector.tensor_scalar(out=C.s_mmu[:], in0=C.psum[0:1, 2048:2304],
                                scalar1=1.0 / HF, scalar2=None, op0=ALU.mult)
        nc.vector.tensor_scalar(out=C.s_mex2[:], in0=C.psum[0:1, 2560:2816],
                                scalar1=1.0 / HF, scalar2=None, op0=ALU.mult)
        nc.vector.tensor_tensor(out=C.s_mtmp[:], in0=C.s_mmu[:], in1=C.s_mmu[:],
                                op=ALU.mult)
        nc.vector.tensor_tensor(out=C.s_mtmp[:], in0=C.s_mex2[:], in1=C.s_mtmp[:],
                                op=ALU.subtract)
        nc.vector.tensor_scalar(out=C.s_mtmp[:], in0=C.s_mtmp[:], scalar1=1e-5,
                                scalar2=None, op0=ALU.add)
        ve.inc(nc.vector.reciprocal(out=C.s_mrin[:], in_=C.s_mtmp[:]))
        st_ev = ve.n
        ve.wait(nc.scalar, st_ev)
        nc.scalar.activation(out=C.s_mrst[:], in_=C.s_mrin[:], func=AF.Sqrt)
        nc.scalar.activation(out=C.s_mnmu[:], in_=C.s_mmu[:], func=AF.Copy, scale=-1.0)
        ac.inc(nc.scalar.activation(out=C.s_mrpk[0:1, :], in_=C.s_mrst[:], func=AF.Copy))
        rows_ac = ac.n
        ac.wait(nc.scalar, rows_ac)
        rc_sc = []
        for t in range(2):
            rc_sc.append(sc.inc(nc.scalar.dma_start(
                out=C.s_mrc[t][:], in_=C.s_mrpk[:, t * 128: t * 128 + 128],
                transpose=True)))
        ve.wait(nc.tensor, st_ev)
        a_done_ve = None
        for t in range(2):
            bA, bB = (2, 3)
            if a_done_ve is not None:
                ve.wait(nc.tensor, a_done_ve)
            for f in range(4):
                nc.tensor.matmul(pk(bA),
                                 C.s_mygs[:, f * 256 + t * 128: f * 256 + t * 128 + 128],
                                 C.s_sw3[:, f * D: f * D + 512],
                                 start=(f == 0), stop=False)
                nc.tensor.matmul(pk(bB),
                                 C.s_mygs[:, f * 256 + t * 128: f * 256 + t * 128 + 128],
                                 C.s_sw3[:, f * D + 512: f * D + 1024],
                                 start=(f == 0), stop=False)
            nc.tensor.matmul(pk(bA), C.s_mnmu[0:1, t * 128: t * 128 + 128],
                             C.s_G1b[0:1, 0:512], start=False, stop=True)
            mm = nc.tensor.matmul(pk(bB), C.s_mnmu[0:1, t * 128: t * 128 + 128],
                                  C.s_G1b[0:1, 512:1024], start=False, stop=True)
            te.inc(mm)
            a_te = te.n
            if t == 0:
                sc.wait(nc.vector, rc_sc)
            te.wait(nc.vector, a_te)
            _vh(C, nc.vector.tensor_copy(out=C.s_mrcf[t][:], in_=C.s_mrc[t][:, 0:1]))
            ve.inc(nc.vector.scalar_tensor_tensor(
                out=C.s_sm[t][:], in0=psum[:, 1024:2048],
                scalar=C.s_mrcf[t][:, 0:1], in1=C.s_B1bc[:],
                op0=ALU.mult, op1=ALU.add))
            a_done_ve = ve.n
        out_ve = ve.n
        ve.wait(nc.tensor, out_ve)
        return out_ve

    ffn_T_shared()

    ve.wait(nc.vector, gates_done)
    for t in range(2):
        nc.vector.memset(C.s_acc[t][:], 0.0)
    ve.inc(nc.vector.memset(C.s_pk128[:], 0.0))
    gp.wait(nc.scalar, C.ew_ld0)
    ew_relay = _age_sc(C)
    ac.wait(nc.tensor, ew_relay)
    gp.wait(nc.tensor, C.ew_ld0)
    ew_free = {}
    for e in range(NE):
        sl = e % 2
        if e >= 2:
            te.wait(nc.gpsimd, ew_free[e - 2])
            ewt = [C.wload(C.s_ew1[sl], C.ti["ew1T"], HF, 8, e),
                   C.wload(C.s_ew2[sl], C.ti["ew2T"], HF, 8, e),
                   C.wload(C.s_ew3[sl], C.ti["ew3T"], D, 4, e)]
            gp.wait(nc.scalar, ewt)
            ew_relay = _age_sc(C)
            ac.wait(nc.tensor, ew_relay)
            gp.wait(nc.tensor, ewt)
        nc.vector.memset(C.s_selbE[:], 0.0)
        for f in range(4):
            inst = nc.vector.tensor_copy(out=C.s_selbE[:, f * 8 + e: f * 8 + e + 1],
                                         in_=C.s_colsb[:, 84 + f:85 + f])
        ve.inc(inst)
        selbe_ve = ve.n
        ve.wait(nc.tensor, selbe_ve)
        for f in range(4):
            for half in range(2):
                nc.tensor.matmul(pk(6 + half, 512, 8), C.s_selbE[:, f * 8: f * 8 + 8],
                                 C.s_ew3[sl][:, f * D + half * 512: f * D + half * 512 + 512],
                                 start=(e == 0 and f == 0), stop=(e == 7 and f == 3),
                                 skip_group_check=True)
        for f in range(4):
            for half in range(2):
                mm = nc.tensor.matmul(pk(2 + half, 512, 1), C.s_colsb[:, 80 + f:81 + f],
                                      C.s_ew3[sl][:, f * D + half * 512: f * D + half * 512 + 512],
                                      start=(f == 0), stop=(f == 3))
        te.inc(mm)
        te.wait(nc.vector)
        _age_ve(C)
        nc.vector.tensor_copy(out=C.s_g1e[0:1, 0:512], in_=psum[0:1, 1024:1536])
        ve.inc(nc.vector.tensor_copy(out=C.s_g1e[0:1, 512:1024],
                                     in_=psum[0:1, 1536:2048]))
        ve.wait(nc.tensor)
        eb1c = [C.s_cols[:, 16 + 4 * e + f: 17 + 4 * e + f] for f in range(4)]
        eb2c = [C.s_cols[:, 48 + 4 * e + f: 49 + 4 * e + f] for f in range(4)]
        ffn_T(C.s_ew1[sl], C.s_ew2[sl], eb1c, eb2c, C.s_ew3[sl], C.s_g1e,
              C.s_moeA, False)
        ew_free[e] = te.n
        for t in range(2):
            _vh(C, nc.vector.tensor_tensor(out=C.s_ohew[t][:, e:e + 1],
                                           in0=C.s_oh[t][:, e:e + 1],
                                           in1=C.s_wex[t][:], op=ALU.mult))
            ve.inc(nc.vector.scalar_tensor_tensor(
                out=C.s_acc[t][:], in0=C.s_moeA[t][:],
                scalar=C.s_ohew[t][:, e:e + 1],
                in1=C.s_acc[t][:], op0=ALU.mult, op1=ALU.add))
    comb_done = ve.n

    te.wait(nc.vector)
    _age_ve(C)
    ve.inc(nc.vector.tensor_tensor(out=C.s_b1e[:], in0=psum[0:8, 3072:4096],
                                   in1=C.s_eb3r[:], op=ALU.add))
    b1e_ve = ve.n

    for t in range(2):
        nc.vector.tensor_copy(out=C.s_pk128[:, 0:8], in_=C.s_ohew[t][:])
        _age_ve(C)
        ve.wait(nc.scalar)
        pkt = sc.inc(nc.scalar.dma_start(out=C.s_pkT[t][:], in_=C.s_pk128[:],
                                         transpose=True))
        sc.wait(nc.tensor, pkt)
        ve.wait(nc.tensor)
        nc.tensor.matmul(pk(0), C.s_pkT[t][0:8, 0:128], C.s_b1e[0:8, 0:512],
                         start=True, stop=True)
        mm = nc.tensor.matmul(pk(1), C.s_pkT[t][0:8, 0:128], C.s_b1e[0:8, 512:1024],
                              start=True, stop=True)
        te.inc(mm)
        te.wait(nc.vector)
        nc.vector.scalar_tensor_tensor(out=C.s_sm[t][:], in0=C.s_sm[t][:],
                                       scalar=C.s_wshm[t][:, 0:1], in1=C.s_acc[t][:],
                                       op0=ALU.mult, op1=ALU.add)
        ve.inc(nc.vector.tensor_tensor(out=C.s_sm[t][:], in0=C.s_sm[t][:],
                                       in1=psum[:, 0:1024], op=ALU.add))
        ve.wait(nc.tensor)
    _age_ve(C)
    delta_ve = ve.n
    ve.wait(nc.gpsimd, delta_ve)
    C.patch_gp = []
    for k, (b0, r) in enumerate(TILE_ORDER):
        srcp = C.s_sm[k // 2][64 * (k % 2): 64 * (k % 2) + 64, :]
        dst = bass.AP(C.i_xp, ((r * B + b0) * 128) * D, [[2 * D, 64], [1, D]])
        C.patch_gp.append(gp.inc(nc.gpsimd.dma_start(out=dst, in_=srcp,
                                                     accum_op=ALU.add)))
    C.moe_te = C.sem["te"].n
    C.moe_ve = C.sem["ve"].n
    C.moe_sc = C.sem["sc"].snap()


def _emit_2B(C, c):
    nc = C.nc
    ld, sc, gp, te, ve, ac, st = (C.sem[n] for n in
                                  ["ld", "sc", "gp", "te", "ve", "ac", "st"])
    pk = C.pk
    psum = C.psum
    SR, umax, nb_of, cum = _cum_of(c)
    amax = umax // 8
    coll = C.sem["coll"]

    te.wait(nc.sync, C.moe_te)
    ve.wait(nc.sync, C.moe_ve)
    sc.wait(nc.sync, C.moe_sc)
    te.wait(nc.vector, C.moe_te)
    te.wait(nc.gpsimd, C.moe_te)

    if C.dbg:
        ve.wait(nc.gpsimd, C.a2_ve)
        C.sem["st"].inc(nc.gpsimd.dma_start(out=C.d_den.ap(), in_=C.s_denom[:]))
        C.sem["st"].wait(nc.vector, C.sem["st"].snap()[-0 - 1])
        for j, v in C.sem["st"].snap():
            if v:
                C.sem["st"].wait(nc.vector, (j, v))
    ve.inc(nc.vector.reciprocal(out=C.s_denom[:, 0:cum[umax + 1]],
                                in_=C.s_denom[:, 0:cum[umax + 1]]))
    denr = ve.n

    coll.wait(nc.gpsimd, C.coll_s)
    ve.wait(nc.tensor, denr)
    for b in range(B):
        if b >= 1:
            te.wait(nc.gpsimd)
        sbtok = gp.inc(nc.gpsimd.dma_start(
            out=C.s_sb2[:], in_=bass.AP(C.i_sg, b * D, [[8 * D, 32], [1, D]])))
        gp.wait(nc.vector, sbtok)
        sb_aged = _age_ve(C)
        ve.wait(nc.tensor, sb_aged)
        gp.wait(nc.tensor, sbtok)
        nc.tensor.matmul(pk(0, 512, 32), C.s_uselp[:, b * 32: b * 32 + 32],
                         C.s_sb2[:, 0:512], start=(b == 0), stop=(b == 7),
                         skip_group_check=True)
        mm = nc.tensor.matmul(pk(1, 512, 32), C.s_uselp[:, b * 32: b * 32 + 32],
                              C.s_sb2[:, 512:1024], start=(b == 0), stop=(b == 7),
                              skip_group_check=True)
        te.inc(_age_te(C) if b == 7 else mm)
    te.wait(nc.scalar)
    ac.inc(nc.scalar.activation(out=C.s_sufb[:], in_=psum[0:32, 0:1024],
                                func=AF.Copy, scale=1e-9))
    sufb_ac = ac.n
    coll.wait(nc.sync, C.coll_v)

    erd_ld, erd_ve, vr_ld, vr_te, att_te = {}, {}, {}, {}, {}
    x2p_ve, ost_st = {}, {}
    fin_ve = ve.n
    av_prev = 0
    for b in range(B):
        if b == 0:
            ac.wait(nc.tensor, sufb_ac)
        else:
            ve.wait(nc.tensor, fin_ve)
            te.wait(nc.vector, av_prev)
        for u in range(umax + 1):
            nb = nb_of(u)
            a = u // 8
            g = b * (amax + 1) + a
            off_g = cum[u] - cum[8 * a]
            if u % 8 == 0:
                if g >= 2:
                    ve.wait(nc.sync, erd_ve[g - 2])
                W = cum[min(8 * a + 7, umax) + 1] - cum[8 * a]
                erd_ld[g] = ld.inc(nc.sync.dma_start(
                    out=C.s_erd[g % 2][:, 0:W],
                    in_=C.i_E.ap()[b][:, cum[8 * a]: cum[8 * a] + W]))
            vdx = len(vr_ld)
            vsl = vdx % 4
            if vdx >= 4:
                te.wait(nc.sync, vr_te[vdx - 4])
            sblk = _blk(u)
            vtok = ld.inc(nc.sync.dma_start(
                out=C.s_vr[vsl][:],
                in_=bass.AP(C.i_gb, C.vg_off + (sblk * 8 + b) * 128 * D, [[D, 128], [1, D]])))
            vr_ld[(b, u)] = (vtok, vdx, vsl)
            if u % 8 == 0:
                ld.wait(nc.vector, erd_ld[g])
                _age_ve(C)
                if u == 0:
                    ve.wait(nc.vector, denr)
            if u >= 2:
                te.wait(nc.vector, att_te[(b, u - 2)])
            vln, vdx2, vsl2 = vr_ld[(b, u)]
            ld.wait(nc.vector, vln)
            _age_ve(C)
            ve.inc(nc.vector.tensor_tensor(
                out=C.s_att[u % 2][:, 0:nb * 128],
                in0=C.s_erd[g % 2][:, off_g: off_g + nb * 128],
                in1=C.s_denom[:, cum[u]: cum[u] + nb * 128], op=ALU.mult))
            if u == SR[nb - 1]:
                nc.vector.tensor_tensor(
                    out=C.s_att[u % 2][:, (nb - 1) * 128: nb * 128],
                    in0=C.s_att[u % 2][:, (nb - 1) * 128: nb * 128],
                    in1=C.s_trilM[:], op=ALU.mult)
                ve.inc(nc.vector.tensor_tensor(
                    out=C.s_att[u % 2][:, (nb - 1) * 128: nb * 128],
                    in0=C.s_att[u % 2][:, (nb - 1) * 128: nb * 128],
                    in1=C.s_trilC[:], op=ALU.add))
            att_rdy = ve.n
            ve.wait(nc.tensor, att_rdy)
            ld.wait(nc.tensor, vln)
            for r in range(nb):
                for half in range(2):
                    mm = nc.tensor.matmul(
                        psum[:, r * 1024 + half * 512: r * 1024 + half * 512 + 512],
                        C.s_att[u % 2][:, r * 128: r * 128 + 128],
                        C.s_vr[vsl2][:, half * 512: half * 512 + 512],
                        start=(u == 0), stop=(u == SR[r]))
            te.inc(mm)
            att_te[(b, u)] = te.n
            vr_te[vdx2] = te.n
            if u == min(8 * a + 7, umax):
                erd_ve[g] = ve.n
        for r in range(4):
            j = b * 4 + r
            for half in range(2):
                mm = nc.tensor.matmul(
                    psum[:, r * 1024 + half * 512: r * 1024 + half * 512 + 512],
                    C.s_sel32[:, j * 128: j * 128 + 128],
                    C.s_sufb[0:32, half * 512: half * 512 + 512],
                    start=False, stop=True)
        te.inc(mm)
        av_te = te.n
        av_prev = av_te
        for r in range(4):
            i = b * 4 + r
            if i >= 2:
                ve.wait(nc.sync, x2p_ve[i - 2])
            if i == 0:
                gp.wait(nc.sync, C.patch_gp)
            xld = ld.inc(nc.sync.dma_start(out=C.s_x2p[r % 2][:],
                                           in_=C.i_xp.ap()[r, b]))
            te.wait(nc.vector, av_te)
            ld.wait(nc.vector, xld)
            _age_ve(C)
            if i >= 2:
                st.wait(nc.vector, ost_st[i - 2])
            nc.vector.tensor_tensor(out=C.s_ost[r % 2][:],
                                    in0=psum[:, r * 1024: r * 1024 + 1024],
                                    in1=C.s_x2p[r % 2][:], op=ALU.add)
            ve.inc(nc.vector.memset(C.s_agev[:], 0.0))
            x2p_ve[i] = ve.n
            ve.wait(nc.gpsimd)
            ost_st[i] = st.inc(nc.gpsimd.dma_start(out=C.t_out.ap()[r, b],
                                                   in_=C.s_ost[r % 2][:]))
        fin_ve = ve.n
    if C.dbg:
        st.inc(nc.gpsimd.dma_start(out=C.d_ks.ap(), in_=C.i_kshard.ap()))
        st.inc(nc.gpsimd.dma_start(out=C.d_vs.ap(), in_=C.i_vshard.ap()))
        st.inc(nc.gpsimd.dma_start(out=C.d_ss.ap(), in_=C.i_sshard.ap()))
    st.wait_all(nc.gpsimd)


def kernel(**inputs):
    in_maps = _host_prepare(inputs)
    nc = _build()
    res = run_bass_kernel_spmd(nc, in_maps, core_ids=list(range(NCORE)))
    out = np.empty((B, 32 * 128, D), np.float32)
    for c in range(NCORE):
        SR = _strips_desc(c)
        arr = np.asarray(res.results[c]["out"])
        for r in range(4):
            out[:, SR[r] * 128:(SR[r] + 1) * 128, :] = arr[r]
    return out
